# version 11 (final)
# speedup vs baseline: 1.1792x; 1.0753x over previous
import math
from contextlib import ExitStack

import numpy as np
import ml_dtypes

import concourse.bass as bass
import concourse.mybir as mybir
import concourse.bass_utils as _bu
from concourse.bass_utils import run_bass_kernel_spmd

_orig_run_command = _bu.run_command


def _run_command_ldwopt(argv, **kw):
    argv = ["--enable-ldw-opt=true" if a == "--enable-ldw-opt=false" else a
            for a in argv]
    return _orig_run_command(argv, **kw)


if _bu.run_command is not _run_command_ldwopt:
    _bu.run_command = _run_command_ldwopt

F32 = mybir.dt.float32
BF16 = mybir.dt.bfloat16
I32 = mybir.dt.int32
AF = mybir.ActivationFunctionType
ALU = mybir.AluOpType
AX = mybir.AxisListType
BF16NP = ml_dtypes.bfloat16

NCORES = 8
D = 4096
S = 2048
B = 4
TPC = 1024
HEAD = 128
NKV = 8
G = 4
SCALE = HEAD ** -0.5
NEG = -1.0e30

KC = D // 128
TG = 512
ts = bass.ts


def _alibi_slopes(n):
    cp2 = 2 ** math.floor(math.log2(n))
    base = 2.0 ** (-8.0 / cp2)
    slopes = base ** np.arange(1, cp2 + 1, dtype=np.float32)
    if cp2 != n:
        extra_base = 2.0 ** (-4.0 / cp2)
        extra = extra_base ** np.arange(1, 1 + 2 * (n - cp2), 2, dtype=np.float32)
        slopes = np.concatenate([slopes, extra])
    return slopes.astype(np.float32)


def build_qkv(fused_qkv):
    nc = bass.Bass("TRN2", target_bir_lowering=False)
    DQ, DKV = D, NKV * HEAD
    xt = nc.dram_tensor("xt", [D, TPC], F32, kind="ExternalInput").ap()
    if fused_qkv:
        wqkv = nc.dram_tensor("w_qkv", [D, DQ + 2 * DKV], F32, kind="ExternalInput").ap()
        wq, wk, wv = wqkv[:, 0:DQ], wqkv[:, DQ:DQ + DKV], wqkv[:, DQ + DKV:DQ + 2 * DKV]
    else:
        wq = nc.dram_tensor("w_q", [D, DQ], F32, kind="ExternalInput").ap()
        wk = nc.dram_tensor("w_k", [D, DKV], F32, kind="ExternalInput").ap()
        wv = nc.dram_tensor("w_v", [D, DKV], F32, kind="ExternalInput").ap()
    q_o = nc.dram_tensor("q", [DQ, TPC], BF16, kind="ExternalOutput").ap()
    k_o = nc.dram_tensor("k", [DKV, TPC], BF16, kind="ExternalOutput").ap()
    v_o = nc.dram_tensor("v", [TPC, DKV], BF16, kind="ExternalOutput").ap()

    n_oc_q, n_oc_k = DQ // 128, DKV // 128
    n_tc = TPC // 128
    n_tg = TPC // TG
    ocs = [("k", i) for i in range(n_oc_k)] + [("q", i) for i in range(n_oc_q)]
    n_oc = len(ocs)

    with ExitStack() as ctx:
        e = ctx.enter_context
        xbf = e(nc.sbuf_tensor("xbf", [128, KC * TPC], BF16))
        wvb = e(nc.sbuf_tensor("wvb", [128, KC * DKV], BF16))
        wbuf = [e(nc.sbuf_tensor(f"wbuf{i}", [128, KC * 128], BF16)) for i in range(2)]
        qstage = [e(nc.sbuf_tensor(f"qstage{i}", [128, TPC], BF16)) for i in range(2)]
        vstage = [e(nc.sbuf_tensor(f"vstage{i}", [128, DKV], BF16)) for i in range(2)]
        ps = [e(nc.psum_tensor(f"ps{i}", [128, TG], F32)) for i in range(4)]
        psv = [e(nc.psum_tensor(f"psv{i}", [128, DKV], F32)) for i in range(2)]
        ldx = e(nc.semaphore("ldx"))
        ldv = e(nc.semaphore("ldv"))
        wl = [e(nc.semaphore(f"wl{i}")) for i in range(2)]
        pe = e(nc.semaphore("pe"))
        ev = e(nc.semaphore("ev"))
        st = [e(nc.semaphore(f"st{i}")) for i in range(2)]
        stv = [e(nc.semaphore(f"stv{i}")) for i in range(2)]
        blk = e(nc.Block())

        x3 = xbf[:].rearrange("p (kc t) -> p kc t", kc=KC)
        wv3 = wvb[:].rearrange("p (kc o) -> p kc o", kc=KC)

        @blk.gpsimd
        def _(g):
            g.dma_start(x3, xt.rearrange("(kc p) t -> p kc t", p=128)).then_inc(ldx, 16)
            g.dma_start(wv3, wv.rearrange("(kc p) o -> p kc o", p=128)).then_inc(ldv, 16)
            for i, (kind, oc) in enumerate(ocs):
                w = wq if kind == "q" else wk
                if i >= 2:
                    g.wait_ge(pe, 2 * (i - 1))
                g.dma_start(
                    wbuf[i % 2][:].rearrange("p (kc o) -> p kc o", kc=KC),
                    w[:, oc * 128:(oc + 1) * 128].rearrange("(kc p) o -> p kc o", p=128),
                ).then_inc(wl[i % 2], 16)

        @blk.tensor
        def _(t):
            t.wait_ge(ldx, 16)
            for i, (kind, oc) in enumerate(ocs):
                t.wait_ge(wl[i % 2], 16 * (i // 2 + 1))
                w3 = wbuf[i % 2][:].rearrange("p (kc o) -> p kc o", kc=KC)
                for tg in range(n_tg):
                    bank = 2 * (i % 2) + tg
                    if i >= 2:
                        t.wait_ge(ev, 2 * (i - 1))
                    for kc in range(KC):
                        ins = t.matmul(ps[bank][:], w3[:, kc, :], x3[:, kc, ts(tg, TG)],
                                       start=(kc == 0), stop=(kc == KC - 1))
                    ins.then_inc(pe, 1)
            t.wait_ge(ldv, 16)
            for tc in range(n_tc):
                if tc >= 2:
                    t.wait_ge(ev, 2 * n_oc + (tc - 1))
                for kc in range(KC):
                    for half in range(2):
                        ins = t.matmul(psv[tc % 2][:, ts(half, TG)],
                                       x3[:, kc, ts(tc, 128)], wv3[:, kc, ts(half, TG)],
                                       start=(kc == 0), stop=(kc == KC - 1))
                ins.then_inc(pe, 1)

        @blk.vector
        def _(v):
            for i in range(n_oc):
                for tg in range(n_tg):
                    bank = 2 * (i % 2) + tg
                    v.wait_ge(pe, 2 * i + tg + 1)
                    if i >= 2:
                        v.wait_ge(st[i % 2], 16 * (i // 2))
                    ins = v.tensor_copy(qstage[i % 2][:, ts(tg, TG)], ps[bank][:])
                    ins.then_inc(ev, 1)
            for tc in range(n_tc):
                v.wait_ge(pe, 2 * n_oc + tc + 1)
                if tc >= 2:
                    v.wait_ge(stv[tc % 2], 16 * (tc // 2))
                v.tensor_copy(vstage[tc % 2][:], psv[tc % 2][:]).then_inc(ev, 1)

        @blk.sync
        def _(sy):
            for i, (kind, oc) in enumerate(ocs):
                sy.wait_ge(ev, 2 * (i + 1))
                dst = q_o if kind == "q" else k_o
                sy.dma_start(dst[oc * 128:(oc + 1) * 128, :], qstage[i % 2][:]).then_inc(st[i % 2], 16)
            for tc in range(n_tc):
                sy.wait_ge(ev, 2 * n_oc + tc + 1)
                sy.dma_start(v_o[tc * 128:(tc + 1) * 128, :], vstage[tc % 2][:]).then_inc(stv[tc % 2], 16)
            sy.wait_ge(st[0], 16 * ((n_oc + 1) // 2))
            sy.wait_ge(st[1], 16 * (n_oc // 2))
            sy.wait_ge(stv[0], 16 * ((n_tc + 1) // 2))
            sy.wait_ge(stv[1], 16 * (n_tc // 2))
    return nc


def build_attn(alibi):
    nc = bass.Bass("TRN2", target_bir_lowering=False)
    NH_L = 16
    NKV_L = 4
    QR = NH_L * HEAD
    KR = NKV_L * HEAD
    n_qg = S // TG
    N_DL = 19

    qg_i = nc.dram_tensor("qg", [QR, S], BF16, kind="ExternalInput").ap()
    kg_i = nc.dram_tensor("kg", [KR, S], BF16, kind="ExternalInput").ap()
    vg_i = nc.dram_tensor("vg", [S, KR], BF16, kind="ExternalInput").ap()
    o_o = nc.dram_tensor("og", [QR, S], BF16, kind="ExternalOutput").ap()
    if alibi:
        sl_i = nc.dram_tensor("slopes", [128, NH_L], F32, kind="ExternalInput").ap()

    items = []
    tile_info = []
    for h in range(NH_L):
        for qg in range(n_qg):
            kcs = list(range(4 * (qg + 1)))
            ti = []
            for kc in kcs:
                masked = alibi or (kc >= 4 * qg)
                ti.append(len(tile_info))
                tile_info.append(dict(h=h, qg=qg, kc=kc, masked=masked,
                                      item=len(items)))
            items.append(dict(h=h, qg=qg, kcs=kcs, tiles=ti))

    P0_HEADS = NKV_L + NH_L
    pe0 = 4 * P0_HEADS + 1
    ve0 = 4 * P0_HEADS
    pe_scores = {}; pe_sum = {}; pe_bcast = {}
    ms_cnt = {}; ex_cnt = {}
    dv_recip = {}; dv_rb = {}; dv_div = {}
    c_pe = pe0; c_ms = 0; c_ex = 0; c_dv = 0
    for si, it in enumerate(items):
        n = len(it["tiles"])
        ops = []
        for j in range(n):
            ops.append(("s", j))
            if j >= 2:
                ops.append(("a", j - 2))
        ops += [("a", n - 2), ("a", n - 1)] if n >= 2 else [("a", 0)]
        it["ops"] = ops
        for kind, j in ops:
            t_ = it["tiles"][j]
            c_pe += 1
            if kind == "s":
                pe_scores[t_] = c_pe
            else:
                pe_sum[t_] = c_pe
        for t_ in it["tiles"]:
            if tile_info[t_]["masked"]:
                c_ms += 1; ms_cnt[t_] = c_ms
            c_ex += 1; ex_cnt[t_] = c_ex
        c_dv += 1; dv_recip[si] = c_dv
        c_pe += 1; pe_bcast[si] = c_pe
        c_dv += 1; dv_rb[si] = c_dv
        c_dv += 1; dv_div[si] = c_dv

    with ExitStack() as ctx:
        e = ctx.enter_context
        q_sb = e(nc.sbuf_tensor("q_sb", [128, NH_L * S], BF16))
        k_sb = e(nc.sbuf_tensor("k_sb", [128, NKV_L * S], BF16))
        v_sb = e(nc.sbuf_tensor("v_sb", [128, NKV_L * S], BF16))
        sqb = [e(nc.sbuf_tensor(f"sqb{i}", [128, S], BF16)) for i in range(2)]
        pr_sb = [e(nc.sbuf_tensor(f"pr{i}", [128, TG], BF16)) for i in range(2)]
        sc_sb = [e(nc.sbuf_tensor(f"sc{i}", [128, TG], F32)) for i in range(2)]
        masks = e(nc.sbuf_tensor("masks", [128, 4 * TG], F32))
        negc_sb = e(nc.sbuf_tensor("negc", [128, NH_L], F32))
        ones_bf = e(nc.sbuf_tensor("ones_bf", [128, 1], BF16))
        ones_rf = e(nc.sbuf_tensor("ones_rf", [1, 128], F32))
        stat = e(nc.sbuf_tensor("stat", [1, 100], F32))
        kmax = e(nc.sbuf_tensor("kmax", [1, NKV_L], F32))
        rrow = e(nc.sbuf_tensor("rrow", [1, TG], F32))
        rb_sb = e(nc.sbuf_tensor("rb", [128, TG], F32))
        ob_sb = [e(nc.sbuf_tensor(f"ob{i}", [128, TG], BF16)) for i in range(2)]
        if alibi:
            sl_sb = e(nc.sbuf_tensor("sl_sb", [128, NH_L], F32))
            sd_sb = e(nc.sbuf_tensor("sd_sb", [128, N_DL * TG], F32))

        ps_s = [e(nc.psum_tensor(f"ps_s{i}", [128, TG], F32)) for i in range(2)]
        ps_o = [e(nc.psum_tensor(f"ps_o{i}", [128, TG], F32)) for i in range(2)]
        ps_r = [e(nc.psum_tensor(f"ps_r{i}", [1, TG], F32)) for i in range(2)]
        ps_b = e(nc.psum_tensor("ps_b", [128, TG], F32))

        ld = e(nc.semaphore("ld"))
        pl = e(nc.semaphore("pl"))
        pre = e(nc.semaphore("pre"))
        ac = e(nc.semaphore("ac"))
        pe = e(nc.semaphore("pe"))
        ve = e(nc.semaphore("ve"))
        ex = e(nc.semaphore("ex"))
        ms = e(nc.semaphore("ms"))
        dv = e(nc.semaphore("dv"))
        st = [e(nc.semaphore(f"st{i}")) for i in range(2)]
        blk = e(nc.Block())

        q3 = q_sb[:].rearrange("p (h t) -> p h t", h=NH_L)
        k3 = k_sb[:].rearrange("p (h t) -> p h t", h=NKV_L)
        v4 = v_sb[:].rearrange("p (h kc d) -> p h kc d", h=NKV_L, kc=S // 128)
        m3 = masks[:].rearrange("p (i t) -> p i t", i=4)
        if alibi:
            sd3 = sd_sb[:].rearrange("p (i t) -> p i t", i=N_DL)
        n_ld = 16 * (4 if alibi else 3)

        n_pl0 = 3 + (N_DL if alibi else 4)
        n_pl1 = n_pl0 + (0 if alibi else 4)

        @blk.gpsimd
        def _(g):
            g.dma_start(q3, qg_i.rearrange("(h p) t -> p h t", p=128)).then_inc(ld, 16)
            g.dma_start(k3, kg_i.rearrange("(h p) t -> p h t", p=128)).then_inc(ld, 16)
            g.dma_start(v4, vg_i.rearrange("(kc p) (h d) -> p h kc d", p=128, d=HEAD)).then_inc(ld, 16)
            if alibi:
                g.dma_start(sl_sb[:], sl_i).then_inc(ld, 16)
            g.memset(ones_bf[:], 1.0).then_inc(pl, 1)
            g.memset(ones_rf[:], 1.0).then_inc(pl, 1)
            g.memset(stat[:], 0.0).then_inc(pl, 1)
            if not alibi:
                for i in range(4):
                    g.memset(m3[:, i, :], 0.0).then_inc(pl, 1)
                g.wait_ge(pl, n_pl0)
                for i in range(4):
                    g.affine_select(m3[:, i, :], m3[:, i, :], pattern=[[1, TG]],
                                    compare_op=ALU.is_ge, fill=NEG,
                                    base=-i * 128, channel_multiplier=-1).then_inc(pl, 1)
            else:
                for dl in range(N_DL):
                    g.iota(sd3[:, dl, :], pattern=[[1, TG]], base=(dl - 3) * 128,
                           channel_multiplier=-1,
                           allow_small_or_imprecise_dtypes=True).then_inc(pl, 1)
                g.wait_ge(pre, 2)
                for dl in range(N_DL):
                    g.affine_select(sd3[:, dl, :], sd3[:, dl, :], pattern=[[1, TG]],
                                    compare_op=ALU.is_ge, fill=4.0e30,
                                    base=(dl - 3) * 128,
                                    channel_multiplier=-1).then_inc(pl, 1)
                g.wait_ge(pl, n_pl0 + N_DL)
                g.memset(stat[:, 99:100], 0.0).then_inc(pre, 1)

        @blk.vector
        def _(v):
            v.wait_ge(ld, n_ld)
            v.wait_ge(pl, n_pl1)
            if alibi:
                for dl in range(N_DL):
                    v.tensor_scalar_max(sd3[:, dl, :], sd3[:, dl, :], 0.0)
            v.memset(kmax[:], 0.0).then_inc(pre, 1)

        @blk.scalar
        def _(s):
            s.wait_ge(pre, 1)
            if alibi:
                for dl in range(N_DL):
                    s.activation(sd3[:, dl, :], sd3[:, dl, :], AF.Sqrt)
            s.copy(stat[:, 98:99], stat[:, 98:99]).then_inc(pre, 1)

        @blk.scalar
        def _(s):
            s.wait_ge(ld, n_ld)
            s.wait_ge(pre, 3 if alibi else 2)
            for h in range(P0_HEADS):
                src = k3[:, h, :] if h < NKV_L else q3[:, h - NKV_L, :]
                if h >= 2:
                    s.wait_ge(pe, 4 * (h - 1))
                s.activation(sqb[h % 2][:], src, AF.Square).then_inc(ac, 1)
            s.wait_ge(pre, 4 if alibi else 3)
            s.activation(stat[:, 80:80 + NH_L], stat[:, 80:80 + NH_L], AF.Sqrt,
                         scale=SCALE * SCALE).then_inc(pre, 1)

        @blk.tensor
        def _(t):
            for h in range(P0_HEADS):
                t.wait_ge(ac, h + 1)
                for qt in range(4):
                    idx = 4 * h + qt
                    if idx >= 2:
                        t.wait_ge(ve, idx - 1)
                    t.matmul(ps_s[qt % 2][0:1, :], ones_bf[:], sqb[h % 2][:, ts(qt, TG)],
                             start=True, stop=True).then_inc(pe, 1)
            t.wait_ge(pre, 5 if alibi else 4)
            t.matmul(ps_o[0][:, 0:NH_L], ones_rf[:], stat[:, 80:80 + NH_L],
                     start=True, stop=True).then_inc(pe, 1)

        @blk.vector
        def _(v):
            for h in range(P0_HEADS):
                for qt in range(4):
                    idx = 4 * h + qt
                    v.wait_ge(pe, idx + 1)
                    v.reduce_max(stat[:, idx:idx + 1], ps_s[qt % 2][0:1, :],
                                 axis=AX.X).then_inc(ve, 1)
            v.wait_ge(ve, 4 * P0_HEADS)
            s4 = stat[:, 0:80].rearrange("o (h q) -> o h q", q=4)
            for h in range(NKV_L):
                v.reduce_max(kmax[:, h:h + 1], s4[:, h, :], axis=AX.X).then_inc(ve, 1)
            for h in range(NH_L):
                v.reduce_max(stat[:, 80 + h:81 + h], s4[:, NKV_L + h, :],
                             axis=AX.X).then_inc(ve, 1)
            v.wait_ge(ve, 4 * P0_HEADS + NKV_L + NH_L)
            for h in range(NH_L):
                kv = (h // G)
                v.tensor_tensor(stat[:, 80 + h:81 + h], stat[:, 80 + h:81 + h],
                                kmax[:, kv:kv + 1], ALU.mult).then_inc(ve, 1)
            v.wait_ge(ve, 4 * P0_HEADS + NKV_L + 2 * NH_L)
            v.memset(rrow[:, 0:1], 0.0).then_inc(pre, 1)
            v.wait_ge(pe, pe0)
            v.tensor_scalar(negc_sb[:], ps_o[0][:, 0:NH_L], -1.0, None,
                            ALU.mult).then_inc(pre, 1)

        PRE_MAIN = 6 if alibi else 5

        @blk.tensor
        def _(t):
            t.wait_ge(pre, PRE_MAIN)
            for si, it in enumerate(items):
                h, qg, kcs, tls = it["h"], it["qg"], it["kcs"], it["tiles"]
                kv = h // G
                nt = len(kcs)
                for kind, j in it["ops"]:
                    kc, t_ = kcs[j], tls[j]
                    if kind == "s":
                        prev = t_ - 2
                        if prev >= 0:
                            ti = tile_info[prev]
                            if ti["masked"]:
                                t.wait_ge(ms, ms_cnt[prev])
                            else:
                                t.wait_ge(ex, ex_cnt[prev])
                        t.matmul(ps_s[t_ % 2][:], k3[:, kv, ts(kc, 128)],
                                 q3[:, h, ts(qg, TG)], start=True,
                                 stop=True).then_inc(pe, 1)
                    else:
                        t.wait_ge(ex, ex_cnt[t_])
                        if j == 0 and si >= 2:
                            t.wait_ge(dv, dv_div[si - 2])
                        t.matmul(ps_o[si % 2][:], v4[:, kv, kc, :], pr_sb[t_ % 2][:],
                                 start=(j == 0), stop=(j == nt - 1))
                        t.matmul(ps_r[si % 2][:], ones_bf[:], pr_sb[t_ % 2][:],
                                 start=(j == 0), stop=(j == nt - 1)).then_inc(pe, 1)
                t.wait_ge(dv, dv_recip[si])
                if si >= 1:
                    t.wait_ge(dv, dv_rb[si - 1])
                t.matmul(ps_b[:], ones_rf[:], rrow[:], start=True, stop=True).then_inc(pe, 1)

        @blk.scalar
        def _(s):
            s.wait_ge(pre, PRE_MAIN)
            for si, it in enumerate(items):
                h = it["h"]
                for j, t_ in enumerate(it["tiles"]):
                    ti = tile_info[t_]
                    prev = t_ - 2
                    if prev >= 0:
                        s.wait_ge(pe, pe_sum[prev])
                    if ti["masked"]:
                        s.wait_ge(ms, ms_cnt[t_])
                        src = sc_sb[t_ % 2][:]
                    else:
                        s.wait_ge(pe, pe_scores[t_])
                        src = ps_s[t_ % 2][:]
                    s.activation(pr_sb[t_ % 2][:], src, AF.Exp,
                                 bias=negc_sb[:, h:h + 1], scale=SCALE).then_inc(ex, 1)

        @blk.vector
        def _(v):
            v.wait_ge(pre, PRE_MAIN)
            for si, it in enumerate(items):
                h, qg = it["h"], it["qg"]
                for j, t_ in enumerate(it["tiles"]):
                    ti = tile_info[t_]
                    if not ti["masked"]:
                        continue
                    kc = ti["kc"]
                    prev = t_ - 2
                    if prev >= 0:
                        v.wait_ge(ex, ex_cnt[prev])
                    v.wait_ge(pe, pe_scores[t_])
                    if alibi:
                        dl = 4 * qg - kc + 3
                        v.scalar_tensor_tensor(sc_sb[t_ % 2][:], sd3[:, dl, :],
                                               sl_sb[:, h:h + 1], ps_s[t_ % 2][:],
                                               ALU.mult, ALU.add).then_inc(ms, 1)
                    else:
                        v.tensor_tensor(sc_sb[t_ % 2][:], ps_s[t_ % 2][:],
                                        m3[:, kc - 4 * qg, :], ALU.add).then_inc(ms, 1)
                v.wait_ge(pe, pe_sum[it["tiles"][-1]])
                v.reciprocal(rrow[:], ps_r[si % 2][:]).then_inc(dv, 1)
                v.wait_ge(pe, pe_bcast[si])
                v.tensor_copy(rb_sb[:], ps_b[:]).then_inc(dv, 1)
                v.wait_ge(dv, dv_rb[si])
                if si >= 2:
                    v.wait_ge(st[si % 2], 16 * (si // 2))
                v.tensor_tensor(ob_sb[si % 2][:], ps_o[si % 2][:], rb_sb[:],
                                ALU.mult).then_inc(dv, 1)

        @blk.sync
        def _(sy):
            for si, it in enumerate(items):
                sy.wait_ge(dv, dv_div[si])
                sy.dma_start(o_o[it["h"] * 128:(it["h"] + 1) * 128, ts(it["qg"], TG)],
                             ob_sb[si % 2][:]).then_inc(st[si % 2], 16)
            sy.wait_ge(st[0], 16 * ((len(items) + 1) // 2))
            sy.wait_ge(st[1], 16 * (len(items) // 2))
    return nc


class _Q:
    def __init__(self):
        self.q = []

    def __call__(self, fn):
        self.q.append(fn)

    def replay(self, eng):
        for fn in self.q:
            fn(eng)


def build_postattn(layer, upto=3):
    nc = bass.Bass("TRN2", target_bir_lowering=False)
    EPS = 1e-5
    attn_i = nc.dram_tensor("attn", [D, TPC], BF16, kind="ExternalInput").ap()
    xt_i = nc.dram_tensor("xt", [D, TPC], F32, kind="ExternalInput").ap()
    wo_i = nc.dram_tensor("w_o", [D, D], F32, kind="ExternalInput").ap()
    if layer == 0:
        w2_i = nc.dram_tensor("w2T", [D, D], I32, kind="ExternalInput").ap()
        wsc_i = nc.dram_tensor("wscale", [1, 1], F32, kind="ExternalInput").ap()
        bias_i = nc.dram_tensor("bias", [D], F32, kind="ExternalInput").ap()
        out_o = nc.dram_tensor("x2", [D, TPC], F32, kind="ExternalOutput").ap()
    else:
        x1_o = nc.dram_tensor("x1", [D, TPC], F32, kind="ExternalOutput").ap()
        x1b_o = nc.dram_tensor("x1b", [D, TPC], BF16, kind="ExternalOutput").ap()

    ident_np = np.eye(128, dtype=np.float32)
    ident_d = nc.inline_tensor(ident_np, "identf").ap()

    with ExitStack() as ctx:
        e = ctx.enter_context
        y3b = e(nc.sbuf_tensor("y3", [128, KC * TG], F32))
        ab = e(nc.sbuf_tensor("attnb", [128, KC * TG], BF16))
        wob = [e(nc.sbuf_tensor(f"wob{i}", [128, KC * 128], BF16)) for i in range(2)]
        xtile = [e(nc.sbuf_tensor(f"xtile{i}", [128, TG], F32)) for i in range(2)]
        ybf = [e(nc.sbuf_tensor(f"ybf{i}", [128, TG], BF16)) for i in range(2)]
        ysq = [e(nc.sbuf_tensor(f"ysq{i}", [128, TG], BF16)) for i in range(2)]
        tno = [e(nc.sbuf_tensor(f"tno{i}", [128, TG], F32)) for i in range(2)]
        ostg = [e(nc.sbuf_tensor(f"ostg{i}", [128, TG], F32)) for i in range(2)]
        rows = e(nc.sbuf_tensor("rows", [1, 8 * TG], F32))
        mb_sb = e(nc.sbuf_tensor("mb_sb", [128, TG], F32))
        rb_sb = e(nc.sbuf_tensor("rb_sb", [128, TG], F32))
        ones_bf = e(nc.sbuf_tensor("ones_bf", [128, 1], BF16))
        ones_rf = e(nc.sbuf_tensor("ones_rf", [1, 128], F32))
        eps_sb = e(nc.sbuf_tensor("eps_sb", [1, 1], F32))
        identf = e(nc.sbuf_tensor("identsb", [128, 128], F32))
        if layer == 0:
            xnb = e(nc.sbuf_tensor("xnb", [128, KC * TG], BF16))
            aab = [e(nc.sbuf_tensor(f"aab{i}", [128, TG], F32)) for i in range(2)]
            w2b = [e(nc.sbuf_tensor(f"w2b{i}", [128, KC * 128], BF16)) for i in range(2)]
            amx = e(nc.sbuf_tensor("amx", [128, TG], F32))
            tmax = e(nc.sbuf_tensor("tmax", [128, 4], F32))
            bias_sb = e(nc.sbuf_tensor("bias_sb", [128, KC], F32))
            wsc_sb = e(nc.sbuf_tensor("wsc_sb", [1, 1], F32))
            ones_rb = e(nc.sbuf_tensor("ones_rb", [1, 128], BF16))
            sb_sb = e(nc.sbuf_tensor("sb_sb", [128, TG], F32))
            rsneg = e(nc.sbuf_tensor("rsneg", [1, TG], BF16))
        else:
            x1bstg = [e(nc.sbuf_tensor(f"x1bstg{i}", [128, TG], BF16)) for i in range(2)]

        psp = [e(nc.psum_tensor(f"psp{i}", [128, TG], F32)) for i in range(2)]
        pss = e(nc.psum_tensor("pss", [1, TG], F32))
        psq = e(nc.psum_tensor("psq", [1, TG], F32))
        psb = [e(nc.psum_tensor(f"psb{i}", [128, TG], F32)) for i in range(2)]
        psr = e(nc.psum_tensor("psr", [1, 128], F32))

        la = e(nc.semaphore("la"))
        lw = [e(nc.semaphore(f"lw{i}")) for i in range(2)]
        lx = [e(nc.semaphore(f"lx{i}")) for i in range(2)]
        l2 = [e(nc.semaphore(f"l2{i}")) for i in range(2)]
        lb = e(nc.semaphore("lb"))
        cb = e(nc.semaphore("cb"))
        pe = e(nc.semaphore("pe"))
        vv = e(nc.semaphore("vv"))
        aa = e(nc.semaphore("aa"))
        st = [e(nc.semaphore(f"st{i}")) for i in range(2)]
        blk = e(nc.Block())

        y33 = y3b[:].rearrange("p (kc t) -> p kc t", kc=KC)
        ab3 = ab[:].rearrange("p (kc t) -> p kc t", kc=KC)
        if layer == 0:
            xn3 = xnb[:].rearrange("p (kc t) -> p kc t", kc=KC)
        R_M, R_V, R_S, R_R, R_AB, R_SE, R_RN, R_MS = range(8)

        def row(i):
            return rows[:, i * TG:(i + 1) * TG]

        Qg, Qs, Qt, Qv, Qa = _Q(), _Q(), _Q(), _Q(), _Q()
        n = dict(pe=0, vv=0, aa=0, la=0, lb=0, lw=[0, 0], lx=[0, 0], l2=[0, 0],
                 st=[0, 0], cb=0)

        def dve(fn, *waits):
            for w in waits:
                Qv(w)
            Qv(fn)
            n["vv"] += 1
            return n["vv"]

        def act(fn, *waits):
            for w in waits:
                Qa(w)
            Qa(fn)
            n["aa"] += 1
            return n["aa"]

        def pe_op(fn, *waits):
            for w in waits:
                Qt(w)
            Qt(fn)
            n["pe"] += 1
            return n["pe"]

        def W(sem, cnt):
            return lambda g: g.wait_ge(sem, cnt)

        Qg(lambda g: g.memset(ones_bf[:], 1.0).then_inc(cb, 1))
        Qg(lambda g: g.memset(ones_rf[:], 1.0).then_inc(cb, 1))
        Qg(lambda g: g.memset(eps_sb[:], 1e-5).then_inc(cb, 1))
        n["cb"] = 3
        if layer == 0:
            Qg(lambda g: g.memset(ones_rb[:], 1.0).then_inc(cb, 1))
            n["cb"] += 1
            Qg(lambda g: g.dma_start(wsc_sb[:], wsc_i).then_inc(lb, 16))
            n["lb"] += 16
            for oc in range(KC):
                Qg(lambda g, oc=oc: g.dma_start(
                    bias_sb[:, oc:oc + 1],
                    bias_i[oc * 128:(oc + 1) * 128].rearrange("(p o) -> p o", o=1),
                ).then_inc(lb, 16))
                n["lb"] += 16
        Qg(lambda g: g.dma_start(identf[:], ident_d).then_inc(lb, 16))
        n["lb"] += 16
        LB = n["lb"]
        CB = n["cb"]

        prev_projmm = {}
        prev_add = {}
        prev_stats = {}
        prev_m1 = {}
        prev_ffnmm = {}
        id_var = [0]

        for hf in range(2):
            ho = hf * TG
            if hf == 1:
                Qs(W(pe, prev_projmm[(0, KC - 1)]))
            Qs(lambda sy, hf=hf: sy.dma_start(
                ab3, attn_i[:, hf * TG:(hf + 1) * TG].rearrange(
                    "(kc p) t -> p kc t", p=128)).then_inc(la, 16))
            n["la"] += 16
            LA = n["la"]

            for oc in range(KC):
                par = oc % 2
                wg = []
                if (hf, oc) in [(0, 0), (0, 1)]:
                    pass
                else:
                    poc = oc - 2
                    phf = hf
                    if poc < 0:
                        poc += KC
                        phf -= 1
                    wg.append(W(pe, prev_projmm[(phf, poc)]))
                for w in wg:
                    Qg(w)
                Qg(lambda g, oc=oc, par=par: g.dma_start(
                    wob[par][:].rearrange("p (kc o) -> p kc o", kc=KC),
                    wo_i[:, oc * 128:(oc + 1) * 128].rearrange("(kc p) o -> p kc o", p=128),
                ).then_inc(lw[par], 16))
                n["lw"][par] += 16
                if (hf, oc) not in [(0, 0), (0, 1)]:
                    poc, phf = oc - 2, hf
                    if poc < 0:
                        poc += KC; phf -= 1
                    Qs(W(vv, prev_add[(phf, poc)]))
                Qs(lambda sy, oc=oc, par=par, hf=hf: sy.dma_start(
                    xtile[par][:],
                    xt_i[oc * 128:(oc + 1) * 128, hf * TG:(hf + 1) * TG],
                ).then_inc(lx[par], 16))
                n["lx"][par] += 16

                waits = [W(lw[par], n["lw"][par]), W(la, LA)]
                if (hf, oc) not in [(0, 0), (0, 1)]:
                    poc, phf = oc - 2, hf
                    if poc < 0:
                        poc += KC; phf -= 1
                    waits.append(W(vv, prev_add[(phf, poc)]))

                def projmm(t, par=par):
                    w3 = wob[par][:].rearrange("p (kc o) -> p kc o", kc=KC)
                    for kc in range(KC):
                        ins = t.matmul(psp[par][:], w3[:, kc, :], ab3[:, kc, :],
                                       start=(kc == 0), stop=(kc == KC - 1))
                    ins.then_inc(pe, 1)
                prev_projmm[(hf, oc)] = pe_op(projmm, *waits)

                waits = [W(pe, prev_projmm[(hf, oc)]), W(lx[par], n["lx"][par])]
                if hf == 1:
                    waits += [W(st[0], n["st"][0]), W(st[1], n["st"][1])]
                ia = dve(lambda v, oc=oc, par=par: v.tensor_tensor(
                    y33[:, oc, :], psp[par][:], xtile[par][:],
                    ALU.add).then_inc(vv, 1), *waits)
                prev_add[(hf, oc)] = ia
                waits = [W(vv, ia)]
                if (hf, oc) not in [(0, 0), (0, 1)]:
                    poc, phf = oc - 2, hf
                    if poc < 0:
                        poc += KC; phf -= 1
                    waits += [W(pe, prev_stats[(phf, poc)])]
                ib = dve(lambda v, oc=oc, par=par: v.tensor_copy(
                    ybf[par][:], y33[:, oc, :]).then_inc(vv, 1), *waits)
                iq = act(lambda s, par=par: s.activation(
                    ysq[par][:], ybf[par][:], AF.Square).then_inc(aa, 1),
                    W(vv, ib))
                waits = [W(vv, ib), W(aa, iq), W(lb, LB), W(cb, CB)]
                if hf == 1 and oc in (0, 1):
                    waits.append(W(vv, id_var[0]))

                def statsmm(t, oc=oc, par=par):
                    t.matmul(pss[:], ones_bf[:], ybf[par][:],
                             start=(oc == 0), stop=(oc == KC - 1))
                    t.matmul(psq[:], ones_bf[:], ysq[par][:],
                             start=(oc == 0), stop=(oc == KC - 1)).then_inc(pe, 1)
                prev_stats[(hf, oc)] = pe_op(statsmm, *waits)

            im = dve(lambda v: v.tensor_scalar(row(R_M), pss[:], 1.0 / D, None,
                                               ALU.mult).then_inc(vv, 1),
                     W(pe, prev_stats[(hf, KC - 1)]))
            i2 = dve(lambda v: v.tensor_tensor(row(R_V), row(R_M), row(R_M),
                                               ALU.mult).then_inc(vv, 1), W(vv, im))
            iv = dve(lambda v: v.scalar_tensor_tensor(row(R_V), psq[:], 1.0 / D,
                                                      row(R_V), ALU.mult,
                                                      ALU.subtract).then_inc(vv, 1),
                     W(vv, i2))
            id_var[0] = iv
            isd = act(lambda s: s.activation(row(R_S), row(R_V), AF.Sqrt,
                                             bias=eps_sb[0:1, 0:1]).then_inc(aa, 1), W(vv, iv))
            ir = dve(lambda v: v.reciprocal(row(R_R), row(R_S)).then_inc(vv, 1),
                     W(aa, isd))
            ibc = pe_op(lambda t: (
                t.matmul(psb[0][:], ones_rf[:], row(R_M), start=True, stop=True),
                t.matmul(psb[1][:], ones_rf[:], row(R_R), start=True,
                         stop=True).then_inc(pe, 1)), W(vv, ir))
            imb = dve(lambda v: v.tensor_copy(mb_sb[:], psb[0][:]).then_inc(vv, 1),
                      W(pe, ibc))
            irb = dve(lambda v: v.tensor_copy(rb_sb[:], psb[1][:]).then_inc(vv, 1))

            id_x1 = {}
            prev_amx = 0
            for oc in range(KC):
                par = oc % 2
                i1 = dve(lambda v, oc=oc, par=par: v.tensor_tensor(
                    tno[par][:], y33[:, oc, :], mb_sb[:],
                    ALU.subtract).then_inc(vv, 1), W(vv, irb))
                i2_ = dve(lambda v, oc=oc, par=par: v.tensor_tensor(
                    y33[:, oc, :], tno[par][:], rb_sb[:],
                    ALU.mult).then_inc(vv, 1), W(vv, i1))
                id_x1[oc] = i2_
                if layer == 0:
                    ag = [W(vv, i2_)]
                    if oc >= 2:
                        ag.append(W(vv, prev_amx))
                    iab = act(lambda s, oc=oc, par=par: s.activation(
                        aab[par][:], y33[:, oc, :], AF.Abs).then_inc(aa, 1), *ag)
                    if oc == 0:
                        prev_amx = dve(lambda v, par=par: v.tensor_copy(
                            amx[:], aab[par][:]).then_inc(vv, 1), W(aa, iab))
                    else:
                        prev_amx = dve(lambda v, par=par: v.tensor_tensor(
                            amx[:], amx[:], aab[par][:], ALU.max).then_inc(vv, 1),
                            W(aa, iab), W(vv, prev_amx))
                else:
                    sg = [W(st[par], n["st"][par])] if n["st"][par] else []
                    ix = dve(lambda v, oc=oc, par=par: v.tensor_copy(
                        x1bstg[par][:], y33[:, oc, :]).then_inc(vv, 1),
                        W(vv, i2_), *sg)
                    Qs(W(vv, ix))
                    Qs(lambda sy, oc=oc, par=par, hf=hf: sy.dma_start(
                        x1b_o[oc * 128:(oc + 1) * 128, hf * TG:(hf + 1) * TG],
                        x1bstg[par][:]).then_inc(st[par], 16))
                    n["st"][par] += 16
                    Qs(lambda sy, oc=oc, par=par, hf=hf: sy.dma_start(
                        x1_o[oc * 128:(oc + 1) * 128, hf * TG:(hf + 1) * TG],
                        y33[:, oc, :]).then_inc(st[par], 16))
                    n["st"][par] += 16

            if layer == 1:
                continue

            if upto == 1:
                for oc in range(KC):
                    par = oc % 2
                    sg = [W(st[par], n["st"][par])] if n["st"][par] else []
                    ic = dve(lambda v, oc=oc, par=par: v.tensor_copy(
                        ostg[par][:], y33[:, oc, :]).then_inc(vv, 1),
                        W(vv, id_x1[oc]), *sg)
                    Qs(W(vv, ic))
                    Qs(lambda sy, oc=oc, par=par, hf=hf: sy.dma_start(
                        out_o[oc * 128:(oc + 1) * 128, hf * TG:(hf + 1) * TG],
                        ostg[par][:]).then_inc(st[par], 16))
                    n["st"][par] += 16
                continue

            id_cp = {}
            for tr in range(4):
                par = tr % 2
                itp = pe_op(lambda t, tr=tr, par=par: t.transpose(
                    psb[par][:, 0:128], amx[:, tr * 128:(tr + 1) * 128],
                    identf[:]).then_inc(pe, 1),
                    W(vv, prev_amx), W(lb, LB), W(cb, CB),
                    *([W(vv, id_cp[tr - 2])] if tr >= 2 else []))
                irm = dve(lambda v, tr=tr, par=par: v.reduce_max(
                    tmax[:, tr:tr + 1], psb[par][:, 0:128],
                    axis=AX.X).then_inc(vv, 1), W(pe, itp))
                id_cp[tr] = irm
            id_rr = {}
            for tr in range(4):
                itp = pe_op(lambda t, tr=tr: t.transpose(
                    psr[:], tmax[:, tr:tr + 1], identf[:]).then_inc(pe, 1),
                    W(vv, id_cp[3]),
                    *([W(vv, id_rr[tr - 1])] if tr >= 1 else []))
                id_rr[tr] = dve(lambda v, tr=tr: v.tensor_copy(
                    row(R_AB)[:, tr * 128:(tr + 1) * 128],
                    psr[:]).then_inc(vv, 1), W(pe, itp))
            icl = dve(lambda v: v.tensor_scalar_max(row(R_AB), row(R_AB),
                                                    EPS).then_inc(vv, 1),
                      W(vv, id_rr[3]))
            ise = dve(lambda v: v.tensor_scalar(row(R_SE), row(R_AB),
                                                wsc_sb[0:1, 0:1], None,
                                                ALU.mult).then_inc(vv, 1),
                      W(vv, icl))
            irn = dve(lambda v: v.reciprocal(row(R_RN), row(R_AB)).then_inc(vv, 1),
                      W(vv, ise))
            ibc2 = pe_op(lambda t: t.matmul(psb[0][:], ones_rf[:], row(R_RN),
                                            start=True, stop=True).then_inc(pe, 1),
                         W(vv, irn))
            irnb = dve(lambda v: v.tensor_copy(mb_sb[:], psb[0][:]).then_inc(vv, 1),
                       W(pe, ibc2))
            id_xn = {}
            for oc in range(KC):
                id_xn[oc] = dve(lambda v, oc=oc: v.tensor_tensor(
                    xn3[:, oc, :], y33[:, oc, :], mb_sb[:],
                    ALU.mult).then_inc(vv, 1), W(vv, irnb))

            if upto == 2:
                for oc in range(KC):
                    par = oc % 2
                    sg = [W(st[par], n["st"][par])] if n["st"][par] else []
                    ic = dve(lambda v, oc=oc, par=par: v.tensor_copy(
                        ostg[par][:], xn3[:, oc, :]).then_inc(vv, 1),
                        W(vv, id_xn[oc]), *sg)
                    Qs(W(vv, ic))
                    Qs(lambda sy, oc=oc, par=par, hf=hf: sy.dma_start(
                        out_o[oc * 128:(oc + 1) * 128, hf * TG:(hf + 1) * TG],
                        ostg[par][:]).then_inc(st[par], 16))
                    n["st"][par] += 16
                continue

            irs = pe_op(lambda t: [
                t.matmul(pss[:], ones_bf[:], xn3[:, kc, :], start=(kc == 0),
                         stop=(kc == KC - 1)) for kc in range(KC)][-1].then_inc(pe, 1),
                W(vv, id_xn[KC - 1]),
                *[W(vv, id_xn[kc]) for kc in range(KC - 1)])
            irsn = dve(lambda v: v.tensor_scalar(rsneg[:], pss[:], -1.5, None,
                                                 ALU.mult).then_inc(vv, 1),
                       W(pe, irs))
            ibc3 = pe_op(lambda t: t.matmul(psb[1][:], ones_rf[:], row(R_SE),
                                            start=True, stop=True).then_inc(pe, 1),
                         W(vv, irsn))
            isb = dve(lambda v: v.tensor_copy(sb_sb[:], psb[1][:]).then_inc(vv, 1),
                      W(pe, ibc3))

            for oc in range(KC):
                par = oc % 2
                wg = []
                if (hf, oc) not in [(0, 0), (0, 1)]:
                    poc, phf = oc - 2, hf
                    if poc < 0:
                        poc += KC; phf -= 1
                    wg.append(W(pe, prev_ffnmm[(phf, poc)]))
                for w in wg:
                    Qg(w)
                Qg(lambda g, oc=oc, par=par: g.dma_start(
                    w2b[par][:].rearrange("p (kc o) -> p kc o", kc=KC),
                    w2_i[:, oc * 128:(oc + 1) * 128].rearrange(
                        "(kc p) o -> p kc o", p=128)).then_inc(l2[par], 16))
                n["l2"][par] += 16

                waits = [W(l2[par], n["l2"][par]), W(vv, isb)]
                if (hf, oc) not in [(0, 0), (0, 1)]:
                    poc, phf = oc - 2, hf
                    if poc < 0:
                        poc += KC; phf -= 1
                    waits.append(W(vv, prev_m1[(phf, poc)]))

                def ffnmm(t, par=par):
                    w3 = w2b[par][:].rearrange("p (kc o) -> p kc o", kc=KC)
                    for kc in range(KC):
                        t.matmul(psp[par][:], w3[:, kc, :], xn3[:, kc, :],
                                 start=(kc == 0), stop=False)
                    t.matmul(psp[par][:], ones_rb[:], rsneg[:],
                             start=False, stop=True).then_inc(pe, 1)
                prev_ffnmm[(hf, oc)] = pe_op(ffnmm, *waits)
                ctx.wgl[par] = prev_ffnmm[(hf, oc)]

                im1 = dve(lambda v, par=par: v.tensor_tensor(
                    tno[par][:], psp[par][:], sb_sb[:], ALU.mult).then_inc(vv, 1),
                    W(pe, prev_ffnmm[(hf, oc)]))
                prev_m1[(hf, oc)] = im1
                iy2 = dve(lambda v, oc=oc, par=par: v.scalar_tensor_tensor(
                    y33[:, oc, :], tno[par][:], bias_sb[:, oc:oc + 1],
                    y33[:, oc, :], ALU.add, ALU.add).then_inc(vv, 1), W(vv, im1))
                waits = [W(vv, iy2)]
                if (hf, oc) not in [(0, 0), (0, 1)]:
                    poc, phf = oc - 2, hf
                    if poc < 0:
                        poc += KC; phf -= 1
                    waits += [W(pe, prev_stats[("f", phf, poc)])]
                ib = dve(lambda v, oc=oc, par=par: v.tensor_copy(
                    ybf[par][:], y33[:, oc, :]).then_inc(vv, 1), *waits)
                iq = act(lambda s, par=par: s.activation(
                    ysq[par][:], ybf[par][:], AF.Square).then_inc(aa, 1), W(vv, ib))
                waits = [W(vv, ib), W(aa, iq)]
                if oc in (0, 1):
                    waits.append(W(vv, id_var[0]))

                def statsmm(t, oc=oc, par=par):
                    t.matmul(pss[:], ones_bf[:], ybf[par][:],
                             start=(oc == 0), stop=(oc == KC - 1))
                    t.matmul(psq[:], ones_bf[:], ysq[par][:],
                             start=(oc == 0), stop=(oc == KC - 1)).then_inc(pe, 1)
                prev_stats[("f", hf, oc)] = pe_op(statsmm, *waits)

            im = dve(lambda v: v.tensor_scalar(row(R_M), pss[:], 1.0 / D, None,
                                               ALU.mult).then_inc(vv, 1),
                     W(pe, prev_stats[("f", hf, KC - 1)]))
            i2 = dve(lambda v: v.tensor_tensor(row(R_V), row(R_M), row(R_M),
                                               ALU.mult).then_inc(vv, 1), W(vv, im))
            iv = dve(lambda v: v.scalar_tensor_tensor(row(R_V), psq[:], 1.0 / D,
                                                      row(R_V), ALU.mult,
                                                      ALU.subtract).then_inc(vv, 1),
                     W(vv, i2))
            id_var[0] = iv
            isd = act(lambda s: s.activation(row(R_S), row(R_V), AF.Sqrt,
                                             bias=eps_sb[0:1, 0:1]).then_inc(aa, 1), W(vv, iv))
            ir = dve(lambda v: v.reciprocal(row(R_R), row(R_S)).then_inc(vv, 1),
                     W(aa, isd))
            ibc = pe_op(lambda t: (
                t.matmul(psb[0][:], ones_rf[:], row(R_M), start=True, stop=True),
                t.matmul(psb[1][:], ones_rf[:], row(R_R), start=True,
                         stop=True).then_inc(pe, 1)), W(vv, ir))
            imb = dve(lambda v: v.tensor_copy(mb_sb[:], psb[0][:]).then_inc(vv, 1),
                      W(pe, ibc))
            irb = dve(lambda v: v.tensor_copy(rb_sb[:], psb[1][:]).then_inc(vv, 1))

            for oc in range(KC):
                par = oc % 2
                i1 = dve(lambda v, oc=oc, par=par: v.tensor_tensor(
                    tno[par][:], y33[:, oc, :], mb_sb[:],
                    ALU.subtract).then_inc(vv, 1), W(vv, irb))
                sg = []
                if n["st"][par] >= 16:
                    sg.append(W(st[par], n["st"][par]))
                i2_ = dve(lambda v, oc=oc, par=par: v.tensor_tensor(
                    ostg[par][:], tno[par][:], rb_sb[:],
                    ALU.mult).then_inc(vv, 1), W(vv, i1), *sg)
                Qs(W(vv, i2_))
                Qs(lambda sy, oc=oc, par=par, hf=hf: sy.dma_start(
                    out_o[oc * 128:(oc + 1) * 128, hf * TG:(hf + 1) * TG],
                    ostg[par][:]).then_inc(st[par], 16))
                n["st"][par] += 16

        Qs(lambda sy: sy.wait_ge(st[0], n["st"][0]))
        Qs(lambda sy: sy.wait_ge(st[1], n["st"][1]))

        @blk.gpsimd
        def _(g):
            Qg.replay(g)

        @blk.sync
        def _(sy):
            Qs.replay(sy)

        @blk.tensor
        def _(t):
            Qt.replay(t)

        @blk.vector
        def _(v):
            Qv.replay(v)

        @blk.scalar
        def _(s):
            Qa.replay(s)
    return nc


def build_moe():
    nc = bass.Bass("TRN2", target_bir_lowering=False)
    INT, LAT = 8192, 2048
    NE = 8
    x1_i = nc.dram_tensor("x1", [D, TPC], F32, kind="ExternalInput").ap()
    x1b_i = nc.dram_tensor("x1b", [D, TPC], BF16, kind="ExternalInput").ap()
    gate_i = nc.dram_tensor("gate_w", [D, NE], F32, kind="ExternalInput").ap()
    up_i = nc.dram_tensor("up_w", [D, INT], F32, kind="ExternalInput").ap()
    down_i = nc.dram_tensor("down_w", [INT, D], F32, kind="ExternalInput").ap()
    fc1_i = nc.dram_tensor("fc1_w", [D, LAT], F32, kind="ExternalInput").ap()
    fc2_i = nc.dram_tensor("fc2_w", [LAT, D], F32, kind="ExternalInput").ap()
    exp_i = nc.dram_tensor("expert_w", [LAT, LAT], F32, kind="ExternalInput").ap()
    gam_i = nc.dram_tensor("ln_gamma", [D], F32, kind="ExternalInput").ap()
    bet_i = nc.dram_tensor("ln_beta", [D], F32, kind="ExternalInput").ap()
    out_o = nc.dram_tensor("out", [D, TPC], F32, kind="ExternalOutput").ap()

    identb_np = np.eye(128, dtype=ml_dtypes.bfloat16)
    identb_d = nc.inline_tensor(identb_np, "identb_d").ap()
    identf_d = nc.inline_tensor(np.eye(128, dtype=np.float32), "identf_d").ap()

    KCL = LAT // 128
    KCI = INT // 128

    with ExitStack() as ctx:
        e = ctx.enter_context
        t3b = e(nc.sbuf_tensor("t3", [128, KC * TG], BF16))
        x1b3b = e(nc.sbuf_tensor("x1b3", [128, KC * TG], BF16))
        h3b = e(nc.sbuf_tensor("h3", [128, KC * TG], BF16))
        lat3b = e(nc.sbuf_tensor("lat3", [128, KCL * TG], BF16))
        rout3b = e(nc.sbuf_tensor("rout3", [128, KCL * TG], BF16))
        wb = [e(nc.sbuf_tensor(f"wb{i}", [128, (KC + KCL) * 128], BF16)) for i in range(2)]
        gate_sb = e(nc.sbuf_tensor("gate_sb", [128, KC * NE], BF16))
        lg_sb = e(nc.sbuf_tensor("lg_sb", [8, S // 4], BF16))
        lt_sb = e(nc.sbuf_tensor("lt_sb", [128, 4 * NE], BF16))
        top_sb = e(nc.sbuf_tensor("top_sb", [128, 8], F32))
        eq_sb = e(nc.sbuf_tensor("eq_sb", [128, NE], F32))
        s2row = e(nc.sbuf_tensor("s2row", [1, TG], F32))
        s2b_sb = e(nc.sbuf_tensor("s2b", [128, TG], F32))
        xtile = [e(nc.sbuf_tensor(f"xtile{i}", [128, TG], F32)) for i in range(2)]
        ybf = [e(nc.sbuf_tensor(f"ybf{i}", [128, TG], BF16)) for i in range(2)]
        ysq = [e(nc.sbuf_tensor(f"ysq{i}", [128, TG], BF16)) for i in range(2)]
        tno = [e(nc.sbuf_tensor(f"tno{i}", [128, TG], F32)) for i in range(2)]
        tn2 = [e(nc.sbuf_tensor(f"tn2{i}", [128, TG], F32)) for i in range(2)]
        ostg = [e(nc.sbuf_tensor(f"ostg{i}", [128, TG], F32)) for i in range(2)]
        rows = e(nc.sbuf_tensor("rows", [1, 4 * TG], F32))
        mb_sb = e(nc.sbuf_tensor("mb_sb", [128, TG], F32))
        rb_sb = e(nc.sbuf_tensor("rb_sb", [128, TG], F32))
        mb2_sb = e(nc.sbuf_tensor("mb2_sb", [128, TG], F32))
        rb2_sb = e(nc.sbuf_tensor("rb2_sb", [128, TG], F32))
        gam_sb = e(nc.sbuf_tensor("gam_sb", [128, KC], F32))
        bet_sb = e(nc.sbuf_tensor("bet_sb", [128, KC], F32))
        ones_bf = e(nc.sbuf_tensor("ones_bf", [128, 1], BF16))
        ones_rf = e(nc.sbuf_tensor("ones_rf", [1, 128], F32))
        eps_sb = e(nc.sbuf_tensor("eps_sb", [1, 1], F32))
        identb = e(nc.sbuf_tensor("identb", [128, 128], BF16))
        identf = e(nc.sbuf_tensor("identf", [128, 128], F32))

        psp = [e(nc.psum_tensor(f"psp{i}", [128, TG], F32)) for i in range(2)]
        pss = e(nc.psum_tensor("pss", [1, TG], F32))
        psq = e(nc.psum_tensor("psq", [1, TG], F32))
        psb = [e(nc.psum_tensor(f"psb{i}", [128, TG], F32)) for i in range(2)]
        psbb = e(nc.psum_tensor("psbb", [128, 16], BF16))

        lw = [e(nc.semaphore(f"lw{i}")) for i in range(2)]
        lx = [e(nc.semaphore(f"lx{i}")) for i in range(2)]
        lb = e(nc.semaphore("lb"))
        la = e(nc.semaphore("la"))
        cb = e(nc.semaphore("cb"))
        pe = e(nc.semaphore("pe"))
        vv = e(nc.semaphore("vv"))
        aa = e(nc.semaphore("aa"))
        st = [e(nc.semaphore(f"st{i}")) for i in range(2)]
        blk = e(nc.Block())

        t33 = t3b[:].rearrange("p (kc t) -> p kc t", kc=KC)
        xb3 = x1b3b[:].rearrange("p (kc t) -> p kc t", kc=KC)
        h33 = h3b[:].rearrange("p (kc t) -> p kc t", kc=KC)
        lat3 = lat3b[:].rearrange("p (kc t) -> p kc t", kc=KCL)
        rout3 = rout3b[:].rearrange("p (kc t) -> p kc t", kc=KCL)
        g3 = gate_sb[:].rearrange("p (kc o) -> p kc o", kc=KC)
        lt3 = lt_sb[:].rearrange("p (qt o) -> p qt o", qt=4)
        R_M, R_V, R_S, R_R = range(4)

        def row(i):
            return rows[:, i * TG:(i + 1) * TG]

        Qg, Qs, Qt, Qv, Qa = _Q(), _Q(), _Q(), _Q(), _Q()
        n = dict(pe=0, vv=0, aa=0, lb=0, cb=0, lw=[0, 0], lx=[0, 0], st=[0, 0])

        def dve(fn, *waits):
            for w in waits:
                Qv(w)
            Qv(fn)
            n["vv"] += 1
            return n["vv"]

        def act(fn, *waits):
            for w in waits:
                Qa(w)
            Qa(fn)
            n["aa"] += 1
            return n["aa"]

        def pe_op(fn, *waits):
            for w in waits:
                Qt(w)
            Qt(fn)
            n["pe"] += 1
            return n["pe"]

        def W(sem, cnt):
            return lambda g: g.wait_ge(sem, cnt)

        for t_, v_ in [(ones_bf, 1.0), (ones_rf, 1.0), (eps_sb, 1e-5)]:
            Qg(lambda g, t_=t_, v_=v_: g.memset(t_[:], v_).then_inc(cb, 1))
            n["cb"] += 1
        CB = n["cb"]
        Qg(lambda g: g.dma_start(identb[:], identb_d).then_inc(lb, 16)); n["lb"] += 16
        Qg(lambda g: g.dma_start(identf[:], identf_d).then_inc(lb, 16)); n["lb"] += 16
        Qg(lambda g: g.dma_start(
            g3, gate_i.rearrange("(kc p) o -> p kc o", p=128)).then_inc(lb, 16))
        n["lb"] += 16
        for oc in range(KC):
            Qg(lambda g, oc=oc: g.dma_start(
                gam_sb[:, oc:oc + 1],
                gam_i[oc * 128:(oc + 1) * 128].rearrange("(p o) -> p o", o=1),
            ).then_inc(lb, 16))
            Qg(lambda g, oc=oc: g.dma_start(
                bet_sb[:, oc:oc + 1],
                bet_i[oc * 128:(oc + 1) * 128].rearrange("(p o) -> p o", o=1),
            ).then_inc(lb, 16))
            n["lb"] += 32
        LB = n["lb"]

        prev = {}
        id_var = [0]
        wbl = [0, 0]
        psl = [0, 0]
        tql = [0, 0]
        xtl = [0, 0]

        for hf in range(2):
            ho = hf * TG

            guards = []
            if hf == 1:
                guards.append(W(pe, prev[("up1", 0, KC - 1)]))
            for w in guards:
                Qs(w)
            Qs(lambda sy, hf=hf: sy.dma_start(
                xb3, x1b_i[:, hf * TG:(hf + 1) * TG].rearrange(
                    "(kc p) t -> p kc t", p=128)).then_inc(la, 16))
            n["la"] = n.get("la", 0) + 16
            LXB = n["la"]

            def gatemm(t):
                for kc in range(KC):
                    ins = t.matmul(psb[0][0:8, :], g3[:, kc, :], xb3[:, kc, :],
                                   start=(kc == 0), stop=(kc == KC - 1))
                ins.then_inc(pe, 1)
            ig = pe_op(gatemm, W(la, LXB), W(lb, LB), W(cb, CB),
                       *([W(vv, prev[("s2b", 0)])] if hf == 1 else []))
            il = dve(lambda v: v.tensor_copy(lg_sb[:], psb[0][0:8, :]).then_inc(vv, 1),
                     W(pe, ig))
            id_s2 = {}
            for qt in range(4):
                itp = pe_op(lambda t, qt=qt: t.transpose(
                    psbb[0:128, 0:8], lg_sb[:, qt * 128:(qt + 1) * 128],
                    identb[0:8, 0:8]).then_inc(pe, 1),
                    W(vv, il), *([W(vv, id_s2[qt - 1])] if qt >= 1 else []))
                i0 = dve(lambda v, qt=qt: v.tensor_copy(
                    lt3[:, qt, :], psbb[0:128, 0:8]).then_inc(vv, 1), W(pe, itp))
                i1 = dve(lambda v, qt=qt: v.reduce_max(
                    top_sb[:, 0:1], lt3[:, qt, :], axis=AX.X).then_inc(vv, 1),
                    W(vv, i0))
                i2 = dve(lambda v, qt=qt: v.tensor_scalar(
                    eq_sb[:], lt3[:, qt, :], top_sb[:, 0:1], None,
                    ALU.is_equal).then_inc(vv, 1), W(vv, i1))
                i3 = dve(lambda v: v.reduce_sum(
                    top_sb[:, 1:2], eq_sb[:], axis=AX.X).then_inc(vv, 1), W(vv, i2))
                i4 = dve(lambda v, qt=qt: v.scalar_tensor_tensor(
                    eq_sb[:], eq_sb[:], NEG, lt3[:, qt, :], ALU.mult,
                    ALU.add).then_inc(vv, 1), W(vv, i3))
                i5 = dve(lambda v: v.reduce_max(
                    top_sb[:, 2:3], eq_sb[:], axis=AX.X).then_inc(vv, 1), W(vv, i4))
                i6 = dve(lambda v: v.tensor_scalar(
                    top_sb[:, 3:4], top_sb[:, 1:2], -1.0, 1.0, ALU.add,
                    ALU.min).then_inc(vv, 1), W(vv, i5))
                i7 = dve(lambda v: v.tensor_tensor(
                    top_sb[:, 4:5], top_sb[:, 0:1], top_sb[:, 2:3],
                    ALU.subtract).then_inc(vv, 1), W(vv, i6))
                i8 = dve(lambda v: v.tensor_tensor(
                    top_sb[:, 5:6], top_sb[:, 3:4], top_sb[:, 4:5],
                    ALU.mult).then_inc(vv, 1), W(vv, i7))
                i9 = dve(lambda v: v.tensor_tensor(
                    top_sb[:, 6:7], top_sb[:, 0:1], top_sb[:, 2:3],
                    ALU.add).then_inc(vv, 1), W(vv, i8))
                i10 = dve(lambda v: v.tensor_tensor(
                    top_sb[:, 7:8], top_sb[:, 6:7], top_sb[:, 5:6],
                    ALU.add).then_inc(vv, 1), W(vv, i9))
                itr = pe_op(lambda t, qt=qt: t.transpose(
                    psb[1][0:1, 128:256], top_sb[:, 7:8],
                    identf[:]).then_inc(pe, 1), W(vv, i10))
                id_s2[qt] = dve(lambda v, qt=qt: v.tensor_copy(
                    s2row[:, qt * 128:(qt + 1) * 128],
                    psb[1][0:1, 128:256]).then_inc(vv, 1), W(pe, itr))
            ibc = pe_op(lambda t: t.matmul(psb[0][:], ones_rf[:], s2row[:],
                                           start=True, stop=True).then_inc(pe, 1),
                        W(vv, id_s2[3]))
            is2b = dve(lambda v: v.tensor_copy(s2b_sb[:], psb[0][:]).then_inc(vv, 1),
                       W(pe, ibc))
            prev[("s2b", 0)] = is2b

            def proj(tag, n_oc, n_kc, w_dram, w_rows, rhs3, evac, extra_mm=None,
                     col_off=0):
                for oc in range(n_oc):
                    par = oc % 2
                    key = (tag, hf, oc)
                    if wbl[par]:
                        Qg(W(pe, wbl[par]))
                    Qg(lambda g, oc=oc, par=par: g.dma_start(
                        wb[par][:, 0:n_kc * 128].rearrange("p (kc o) -> p kc o", kc=n_kc),
                        w_dram[w_rows:w_rows + n_kc * 128,
                               col_off + oc * 128:col_off + (oc + 1) * 128].rearrange(
                            "(kc p) o -> p kc o", p=128)).then_inc(lw[par], 16))
                    n["lw"][par] += 16

                    waits = [W(lw[par], n["lw"][par])]
                    if psl[par]:
                        waits.append(W(vv, psl[par]))

                    def mm(t, oc=oc, par=par):
                        w3 = wb[par][:, 0:n_kc * 128].rearrange(
                            "p (kc o) -> p kc o", kc=n_kc)
                        for kc in range(n_kc):
                            ins = t.matmul(psp[par][:], w3[:, kc, :], rhs3[:, kc, :],
                                           start=(kc == 0),
                                           stop=(kc == n_kc - 1 and extra_mm is None))
                        if extra_mm is not None:
                            ins = extra_mm(t, oc, par)
                        ins.then_inc(pe, 1)
                    prev[key] = pe_op(mm, *waits)
                    wbl[par] = prev[key]
                    psl[par] = evac(oc, par, prev[key])
                    prev[("e" + tag, hf, oc)] = psl[par]

            def ev_fc1(oc, par, pemm):
                return dve(lambda v, oc=oc, par=par: v.tensor_tensor(
                    lat3[:, oc, :], psp[par][:], s2b_sb[:],
                    ALU.mult).then_inc(vv, 1), W(pe, pemm), W(vv, is2b))
            proj("fc1", KCL, KC, fc1_i, 0, xb3, ev_fc1)

            def ev_exp(oc, par, pemm):
                return dve(lambda v, oc=oc, par=par: v.tensor_copy(
                    rout3[:, oc, :], psp[par][:]).then_inc(vv, 1), W(pe, pemm))
            lat_ready = [W(vv, prev[("efc1", hf, oc)]) for oc in range(KCL)]
            for w in lat_ready:
                Qt(w)
            proj("exp", KCL, KCL, exp_i, 0, lat3, ev_exp)

            for hp in range(2):
                def ev_up(oc, par, pemm, hp=hp):
                    guards = [W(pe, pemm)]
                    if tql[par]:
                        guards.append(W(aa, tql[par]))
                    ir = dve(lambda v, oc=oc, par=par: v.tensor_scalar_max(
                        tno[par][:], psp[par][:], 0.0).then_inc(vv, 1), *guards)
                    iq = act(lambda s, oc=oc, par=par: s.activation(
                        h33[:, oc, :], tno[par][:], AF.Square).then_inc(aa, 1),
                        W(vv, ir))
                    prev[("hq", hp, hf, oc)] = iq
                    tql[par] = iq
                    return ir
                proj(f"up{hp}", KC, KC, up_i, 0, xb3, ev_up, col_off=hp * KC * 128)

                hq_ready = [W(aa, prev[("hq", hp, hf, oc)]) for oc in range(KC)]
                for w in hq_ready:
                    Qt(w)

                if hp == 0:
                    def ev_dn(oc, par, pemm):
                        return dve(lambda v, oc=oc, par=par: v.tensor_copy(
                            t33[:, oc, :], psp[par][:]).then_inc(vv, 1), W(pe, pemm))
                    proj("dn0", KC, KC, down_i, 0, h33, ev_dn)
                else:
                    def extra_fc2(t, oc, par):
                        w3b = wb[par][:, KC * 128:(KC + KCL) * 128].rearrange(
                            "p (kc o) -> p kc o", kc=KCL)
                        for kc in range(KCL):
                            ins = t.matmul(psp[par][:], w3b[:, kc, :], rout3[:, kc, :],
                                           start=False, stop=(kc == KCL - 1))
                        return ins

                    for oc in range(KC):
                        par = oc % 2
                        if wbl[par]:
                            Qg(W(pe, wbl[par]))
                        Qg(lambda g, oc=oc, par=par: g.dma_start(
                            wb[par][:, 0:KC * 128].rearrange("p (kc o) -> p kc o", kc=KC),
                            down_i[KC * 128:2 * KC * 128,
                                   oc * 128:(oc + 1) * 128].rearrange(
                                "(kc p) o -> p kc o", p=128)).then_inc(lw[par], 16))
                        n["lw"][par] += 16
                        Qg(lambda g, oc=oc, par=par: g.dma_start(
                            wb[par][:, KC * 128:(KC + KCL) * 128].rearrange(
                                "p (kc o) -> p kc o", kc=KCL),
                            fc2_i[:, oc * 128:(oc + 1) * 128].rearrange(
                                "(kc p) o -> p kc o", p=128)).then_inc(lw[par], 16))
                        n["lw"][par] += 16

                        waits = [W(lw[par], n["lw"][par])]
                        if psl[par]:
                            waits.append(W(vv, psl[par]))
                        rt_ready = [W(vv, prev[("eexp", hf, k)]) for k in range(KCL)] \
                            if oc == 0 else []

                        def mm(t, oc=oc, par=par):
                            w3 = wb[par][:, 0:KC * 128].rearrange(
                                "p (kc o) -> p kc o", kc=KC)
                            for kc in range(KC):
                                t.matmul(psp[par][:], w3[:, kc, :], h33[:, kc, :],
                                         start=(kc == 0), stop=False)
                            extra_fc2(t, oc, par).then_inc(pe, 1)
                        prev[("dn1", hf, oc)] = pe_op(mm, *waits, *rt_ready)
                        wbl[par] = prev[("dn1", hf, oc)]

                        ita = dve(lambda v, oc=oc, par=par: v.tensor_tensor(
                            t33[:, oc, :], psp[par][:], t33[:, oc, :],
                            ALU.add).then_inc(vv, 1),
                            W(pe, prev[("dn1", hf, oc)]),
                            W(vv, prev[("edn0", hf, oc)]))
                        prev[("tadd", hf, oc)] = ita
                        psl[par] = ita
                        iq = act(lambda s, oc=oc, par=par: s.activation(
                            ysq[par][:], t33[:, oc, :], AF.Square).then_inc(aa, 1),
                            W(vv, ita),
                            *([W(pe, prev[("tst", hf, oc - 2)])] if oc >= 2 else
                              ([W(pe, prev[("tst", 0, KC - 2 + oc)])] if hf == 1 else [])))
                        waits = [W(vv, ita), W(aa, iq)]
                        if oc in (0, 1) and hf == 1:
                            waits.append(W(vv, id_var[0]))

                        def statsmm(t, oc=oc, par=par):
                            t.matmul(pss[:], ones_bf[:], t33[:, oc, :],
                                     start=(oc == 0), stop=(oc == KC - 1))
                            t.matmul(psq[:], ones_bf[:], ysq[par][:],
                                     start=(oc == 0), stop=(oc == KC - 1)).then_inc(pe, 1)
                        prev[("tst", hf, oc)] = pe_op(statsmm, *waits)

            def rows_pass(dst_m, dst_r):
                im = dve(lambda v: v.tensor_scalar(row(R_M), pss[:], 1.0 / D, None,
                                                   ALU.mult).then_inc(vv, 1))
                i2 = dve(lambda v: v.tensor_tensor(row(R_V), row(R_M), row(R_M),
                                                   ALU.mult).then_inc(vv, 1),
                         W(vv, im))
                iv = dve(lambda v: v.scalar_tensor_tensor(
                    row(R_V), psq[:], 1.0 / D, row(R_V), ALU.mult,
                    ALU.subtract).then_inc(vv, 1), W(vv, i2))
                isd = act(lambda s: s.activation(row(R_S), row(R_V), AF.Sqrt,
                                                 bias=eps_sb[0:1, 0:1]).then_inc(aa, 1),
                          W(vv, iv))
                ir = dve(lambda v: v.reciprocal(row(R_R), row(R_S)).then_inc(vv, 1),
                         W(aa, isd))
                ibc = pe_op(lambda t: (
                    t.matmul(psb[0][:], ones_rf[:], row(R_M), start=True, stop=True),
                    t.matmul(psb[1][:], ones_rf[:], row(R_R), start=True,
                             stop=True).then_inc(pe, 1)), W(vv, ir))
                imb = dve(lambda v: v.tensor_copy(dst_m[:], psb[0][:]).then_inc(vv, 1),
                          W(pe, ibc))
                irb = dve(lambda v: v.tensor_copy(dst_r[:], psb[1][:]).then_inc(vv, 1))
                return iv, irb

            Qv(W(pe, prev[("tst", hf, KC - 1)]))
            ivr, irb = rows_pass(mb_sb, rb_sb)

            def lnout_u(oc, par, dep):
                i1 = dve(lambda v, oc=oc, par=par: v.tensor_tensor(
                    tno[par][:], t33[:, oc, :], mb_sb[:],
                    ALU.subtract).then_inc(vv, 1), *dep)
                i2 = dve(lambda v, oc=oc, par=par: v.tensor_tensor(
                    tno[par][:], tno[par][:], rb_sb[:],
                    ALU.mult).then_inc(vv, 1), W(vv, i1))
                i3 = dve(lambda v, oc=oc, par=par: v.tensor_scalar(
                    tno[par][:], tno[par][:], gam_sb[:, oc:oc + 1],
                    bet_sb[:, oc:oc + 1], ALU.mult, ALU.add).then_inc(vv, 1),
                    W(vv, i2))
                i4 = dve(lambda v, oc=oc, par=par: v.tensor_tensor(
                    tn2[par][:], tno[par][:], xtile[par][:],
                    ALU.add).then_inc(vv, 1), W(vv, i3), W(lx[par], n["lx"][par]))
                xtl[par] = i4
                return i4

            for oc in range(KC):
                par = oc % 2
                if xtl[par]:
                    Qs(W(vv, xtl[par]))
                Qs(lambda sy, oc=oc, par=par, hf=hf: sy.dma_start(
                    xtile[par][:],
                    x1_i[oc * 128:(oc + 1) * 128, hf * TG:(hf + 1) * TG],
                ).then_inc(lx[par], 16))
                n["lx"][par] += 16
                iu = lnout_u(oc, par, [W(vv, irb)])
                prev[("u1", hf, oc)] = iu
                waits = [W(vv, iu)]
                if oc >= 2:
                    waits.append(W(pe, prev[("ust", hf, oc - 2)]))
                elif hf == 1:
                    waits.append(W(pe, prev[("ust", 0, KC - 2 + oc)]))
                ib = dve(lambda v, oc=oc, par=par: v.tensor_copy(
                    ybf[par][:], tn2[par][:]).then_inc(vv, 1), *waits)
                iq = act(lambda s, par=par: s.activation(
                    ysq[par][:], ybf[par][:], AF.Square).then_inc(aa, 1), W(vv, ib))
                waits = [W(vv, ib), W(aa, iq)]
                if oc in (0, 1):
                    waits.append(W(vv, ivr))
                def statsmm(t, oc=oc, par=par):
                    t.matmul(pss[:], ones_bf[:], ybf[par][:],
                             start=(oc == 0), stop=(oc == KC - 1))
                    t.matmul(psq[:], ones_bf[:], ysq[par][:],
                             start=(oc == 0), stop=(oc == KC - 1)).then_inc(pe, 1)
                prev[("ust", hf, oc)] = pe_op(statsmm, *waits)

            Qv(W(pe, prev[("ust", hf, KC - 1)]))
            iv2, irb2 = rows_pass(mb2_sb, rb2_sb)
            id_var[0] = iv2

            for oc in range(KC):
                par = oc % 2
                if xtl[par]:
                    Qs(W(vv, xtl[par]))
                Qs(lambda sy, oc=oc, par=par, hf=hf: sy.dma_start(
                    xtile[par][:],
                    x1_i[oc * 128:(oc + 1) * 128, hf * TG:(hf + 1) * TG],
                ).then_inc(lx[par], 16))
                n["lx"][par] += 16
                iu = lnout_u(oc, par, [W(vv, irb2)])
                prev[("u2", hf, oc)] = iu
                i5 = dve(lambda v, oc=oc, par=par: v.tensor_tensor(
                    tno[par][:], tn2[par][:], mb2_sb[:],
                    ALU.subtract).then_inc(vv, 1), W(vv, iu))
                sg = [W(st[par], n["st"][par])] if n["st"][par] else []
                i6 = dve(lambda v, oc=oc, par=par: v.tensor_tensor(
                    ostg[par][:], tno[par][:], rb2_sb[:],
                    ALU.mult).then_inc(vv, 1), W(vv, i5), *sg)
                Qs(W(vv, i6))
                Qs(lambda sy, oc=oc, par=par, hf=hf: sy.dma_start(
                    out_o[oc * 128:(oc + 1) * 128, hf * TG:(hf + 1) * TG],
                    ostg[par][:]).then_inc(st[par], 16))
                n["st"][par] += 16

        Qs(lambda sy: sy.wait_ge(st[0], n["st"][0]))
        Qs(lambda sy: sy.wait_ge(st[1], n["st"][1]))

        @blk.gpsimd
        def _(g):
            Qg.replay(g)

        @blk.sync
        def _(sy):
            Qs.replay(sy)

        @blk.tensor
        def _(t):
            Qt.replay(t)

        @blk.vector
        def _(v):
            Qv.replay(v)

        @blk.scalar
        def _(s):
            Qa.replay(s)
    return nc


_NC_CACHE = {}


def _get_nc(name):
    if name not in _NC_CACHE:
        _NC_CACHE[name] = {
            "qkv0": lambda: build_qkv(False),
            "qkv1": lambda: build_qkv(True),
            "attn0": lambda: build_attn(False),
            "attn1": lambda: build_attn(True),
            "post0": lambda: build_postattn(0),
            "post1": lambda: build_postattn(1),
            "moe": lambda: build_moe(),
        }[name]()
    return _NC_CACHE[name]


def _run(name, in_maps):
    nc = _get_nc(name)
    return run_bass_kernel_spmd(nc, in_maps, core_ids=list(range(NCORES))).results


def _attn_stage(res_qkv, alibi, w_extra=None):
    QF, KF, VT = {}, {}, {}
    for b in range(B):
        QF[b] = np.concatenate([np.asarray(res_qkv[2 * b]["q"]),
                                np.asarray(res_qkv[2 * b + 1]["q"])], axis=1)
        KF[b] = np.concatenate([np.asarray(res_qkv[2 * b]["k"]),
                                np.asarray(res_qkv[2 * b + 1]["k"])], axis=1)
        VT[b] = np.concatenate([np.asarray(res_qkv[2 * b]["v"]),
                                np.asarray(res_qkv[2 * b + 1]["v"])], axis=0)
    in_maps = []
    for c in range(NCORES):
        b, h4 = c // 2, c % 2
        m = {
            "qg": np.ascontiguousarray(QF[b][h4 * 2048:(h4 + 1) * 2048]),
            "kg": np.ascontiguousarray(KF[b][h4 * 512:(h4 + 1) * 512]),
            "vg": np.ascontiguousarray(VT[b][:, h4 * 512:(h4 + 1) * 512]),
        }
        if alibi:
            sl = _alibi_slopes(32).reshape(NKV, G)[4 * h4:4 * h4 + 4].reshape(-1)
            m["slopes"] = np.ascontiguousarray(
                np.broadcast_to((-sl * math.sqrt(HEAD)).astype(np.float32),
                                (128, 16)))
        in_maps.append(m)
    res = _run("attn1" if alibi else "attn0", in_maps)
    ATT = {}
    for b in range(B):
        ATT[b] = np.concatenate([np.asarray(res[2 * b]["og"]),
                                 np.asarray(res[2 * b + 1]["og"])], axis=0)
    return ATT


def kernel_unfused(L0_attn_hidden_states, L0_attn_w_q, L0_attn_w_k, L0_attn_w_v,
           L0_attn_w_o, L0_ffn_weight_2bit, L0_ffn_weight_scale, L0_ffn_bias,
           L1_attn_w_qkv, L1_attn_w_o, L1_ffn_gate_w, L1_ffn_up_w,
           L1_ffn_down_w, L1_ffn_fc1_w, L1_ffn_fc2_w, L1_ffn_expert_w,
           L1_ffn_ln_gamma, L1_ffn_ln_beta):
    x = np.asarray(L0_attn_hidden_states)
    f = np.asarray
    xt = [np.ascontiguousarray(x[c // 2, (c % 2) * TPC:(c % 2 + 1) * TPC, :].T)
          for c in range(NCORES)]

    resA = _run("qkv0", [{"xt": xt[c], "w_q": f(L0_attn_w_q),
                          "w_k": f(L0_attn_w_k), "w_v": f(L0_attn_w_v)}
                         for c in range(NCORES)])
    ATT0 = _attn_stage(resA, alibi=False)

    w2T = np.ascontiguousarray(np.asarray(L0_ffn_weight_2bit).T)
    in_maps = []
    for c in range(NCORES):
        b, h = c // 2, c % 2
        in_maps.append({
            "attn": np.ascontiguousarray(ATT0[b][:, h * TPC:(h + 1) * TPC]),
            "xt": xt[c],
            "w_o": f(L0_attn_w_o),
            "w2T": w2T,
            "wscale": f(L0_ffn_weight_scale).reshape(1, 1),
            "bias": f(L0_ffn_bias),
        })
    resC = _run("post0", in_maps)
    x2 = [np.asarray(resC[c]["x2"]) for c in range(NCORES)]

    resD = _run("qkv1", [{"xt": x2[c], "w_qkv": f(L1_attn_w_qkv)}
                         for c in range(NCORES)])
    ATT1 = _attn_stage(resD, alibi=True)

    in_maps = []
    for c in range(NCORES):
        b, h = c // 2, c % 2
        in_maps.append({
            "attn": np.ascontiguousarray(ATT1[b][:, h * TPC:(h + 1) * TPC]),
            "xt": x2[c],
            "w_o": f(L1_attn_w_o),
        })
    resF1 = _run("post1", in_maps)

    in_maps = []
    for c in range(NCORES):
        in_maps.append({
            "x1": np.asarray(resF1[c]["x1"]),
            "x1b": np.asarray(resF1[c]["x1b"]),
            "gate_w": f(L1_ffn_gate_w),
            "up_w": f(L1_ffn_up_w),
            "down_w": f(L1_ffn_down_w),
            "fc1_w": f(L1_ffn_fc1_w),
            "fc2_w": f(L1_ffn_fc2_w),
            "expert_w": f(L1_ffn_expert_w),
            "ln_gamma": f(L1_ffn_ln_gamma),
            "ln_beta": f(L1_ffn_ln_beta),
        })
    resF2 = _run("moe", in_maps)

    out = np.empty((B, S, D), dtype=np.float32)
    for c in range(NCORES):
        b, h = c // 2, c % 2
        out[b, h * TPC:(h + 1) * TPC, :] = np.asarray(resF2[c]["out"]).T
    return out


class MCtx:

    def __init__(self, nc, e):
        self.nc = nc
        self.e = e
        self.Qg, self.Qs, self.Qt, self.Qv, self.Qa = _Q(), _Q(), _Q(), _Q(), _Q()
        self.wg = [e(nc.semaphore(f"m_wg{i}")) for i in range(2)]
        self.lg = e(nc.semaphore("m_lg"))
        self.ls = e(nc.semaphore("m_ls"))
        self.lx = [e(nc.semaphore(f"m_lx{i}")) for i in range(4)]
        self.st = [e(nc.semaphore(f"m_st{i}")) for i in range(2)]
        self.pe = e(nc.semaphore("m_pe"))
        self.vv = e(nc.semaphore("m_vv"))
        self.aa = e(nc.semaphore("m_aa"))
        self.cb = e(nc.semaphore("m_cb"))
        self.cs = e(nc.semaphore("m_cs"))
        self.ex = e(nc.semaphore("m_ex"))
        self.bar = e(nc.semaphore("m_bar"))
        self.n = dict(pe=0, vv=0, aa=0, cb=0, cs=0, lg=0, ls=0, bar=0,
                      wg=[0, 0], lx=[0, 0, 0, 0], st=[0, 0])
        self.wgl = [0, 0]

    def dve(self, fn, *waits):
        for w in waits:
            self.Qv(w)
        self.Qv(fn)
        self.n["vv"] += 1
        return self.n["vv"]

    def act(self, fn, *waits):
        for w in waits:
            self.Qa(w)
        self.Qa(fn)
        self.n["aa"] += 1
        return self.n["aa"]

    def pe_op(self, fn, *waits):
        for w in waits:
            self.Qt(w)
        self.Qt(fn)
        self.n["pe"] += 1
        return self.n["pe"]

    def barrier(self):
        self.n["bar"] += 1
        tgt = 5 * self.n["bar"]
        for q in (self.Qg, self.Qs, self.Qt, self.Qv, self.Qa):
            q(lambda g: g.sem_inc(self.bar, 1))
            q(lambda g: g.wait_ge(self.bar, tgt))

    def replay(self, blk):
        @blk.gpsimd
        def _(g):
            self.Qg.replay(g)

        @blk.sync
        def _(sy):
            self.Qs.replay(sy)

        @blk.tensor
        def _(t):
            self.Qt.replay(t)

        @blk.vector
        def _(v):
            self.Qv.replay(v)

        @blk.scalar
        def _(s):
            self.Qa.replay(s)


def W(sem, cnt):
    return lambda g: g.wait_ge(sem, cnt)


def emit_qkv_phase(ctx, cst, xt_ap, x_is_f32, wq, wk, wv, q_dr, ck, cv,
                   ag=None):
    nc, e, n = ctx.nc, ctx.e, ctx.n
    _u = 'u%d' % n.setdefault('uid', 0); n['uid'] += 1
    DQ, DKV = D, NKV * HEAD
    n_oc_q, n_oc_k, n_tc = DQ // 128, DKV // 128, TPC // 128
    ocs = [("k", i) for i in range(n_oc_k)] + [("q", i) for i in range(n_oc_q)]
    with ExitStack() as px:
        pe_ = px.enter_context
        xbf = pe_(nc.sbuf_tensor(_u + "qk_xbf", [128, KC * TPC], BF16))
        wvb = pe_(nc.sbuf_tensor(_u + "qk_wvb", [128, KC * DKV], BF16))
        wbuf = [pe_(nc.sbuf_tensor(_u + f"qk_wb{i}", [128, KC * 128], BF16)) for i in range(2)]
        qstage = [pe_(nc.sbuf_tensor(_u + f"qk_qs{i}", [128, TPC], BF16)) for i in range(2)]
        vstage = [pe_(nc.sbuf_tensor(_u + f"qk_vs{i}", [128, DKV], BF16)) for i in range(2)]
        psp = [pe_(nc.psum_tensor(_u + f"qk_ps{i}", [128, TG], F32)) for i in range(4)]
        psv = [pe_(nc.psum_tensor(_u + f"qk_pv{i}", [128, DKV], F32)) for i in range(2)]

        x3 = xbf[:].rearrange("p (kc t) -> p kc t", kc=KC)
        wv3 = wvb[:].rearrange("p (kc o) -> p kc o", kc=KC)

        ctx.Qg(lambda g: g.dma_start(
            x3, xt_ap.rearrange("(kc p) t -> p kc t", p=128)).then_inc(ctx.lg, 16))
        n["lg"] += 16
        LX16 = n["lg"]

        pe_mm = {}
        ev_id = {}
        st_base = [n["st"][0], n["st"][1]]
        CBX = [0]
        CBV = [0]

        def emit_item(i, kind, oc):
            par = i % 2
            if ctx.wgl[par]:
                ctx.Qg(W(ctx.pe, ctx.wgl[par]))
            w = wq if kind == "q" else wk
            ctx.Qg(lambda g, oc=oc, par=par, w=w: g.dma_start(
                wbuf[par][:].rearrange("p (kc o) -> p kc o", kc=KC),
                w[:, oc * 128:(oc + 1) * 128].rearrange("(kc p) o -> p kc o", p=128),
            ).then_inc(ctx.wg[par], 16))
            n["wg"][par] += 16
            if i == 1:
                ctx.Qg(W(ctx.lg, LX16))
                ctx.Qg(lambda g: g.sem_inc(ctx.cb, 1))
                n["cb"] += 1
                CBX[0] = n["cb"]
                ctx.Qg(lambda g: g.dma_start(
                    wv3, wv.rearrange("(kc p) o -> p kc o", p=128)).then_inc(
                    ctx.lg, 16))
                n["lg"] += 16
                CBV[1 if False else 0] = 0
                LV16 = n["lg"]
                CBX.append(LV16)
            if i == 3:
                ctx.Qg(W(ctx.lg, CBX[1]))
                ctx.Qg(lambda g: g.sem_inc(ctx.cb, 1))
                n["cb"] += 1
                CBV[0] = n["cb"]

            waits = [W(ctx.wg[par], n["wg"][par]),
                     (lambda g, c=CBX: g.wait_ge(ctx.cb, c[0]))]
            if i >= 2:
                waits.append(W(ctx.vv, ev_id[i - 2]))

            def mm(t, par=par):
                w3 = wbuf[par][:].rearrange("p (kc o) -> p kc o", kc=KC)
                for tg in range(2):
                    for kc in range(KC):
                        ins = t.matmul(psp[2 * par + tg][:], w3[:, kc, :],
                                       x3[:, kc, ts(tg, TG)],
                                       start=(kc == 0), stop=(kc == KC - 1))
                ins.then_inc(ctx.pe, 1)
            pe_mm[i] = ctx.pe_op(mm, *waits)
            ctx.wgl[par] = pe_mm[i]

            sg = [W(ctx.st[par], n["st"][par])] if n["st"][par] > st_base[par] else []
            i1 = ctx.dve(lambda v, par=par: v.tensor_copy(
                qstage[par][:, 0:TG], psp[2 * par][:]).then_inc(ctx.vv, 1),
                W(ctx.pe, pe_mm[i]), *sg)
            i2 = ctx.dve(lambda v, par=par: v.tensor_copy(
                qstage[par][:, TG:TPC], psp[2 * par + 1][:]).then_inc(ctx.vv, 1))
            ev_id[i] = i2
            dst = q_dr if kind == "q" else ck
            ctx.Qs(W(ctx.vv, i2))
            ctx.Qs(lambda sy, oc=oc, par=par, dst=dst: sy.dma_start(
                dst[oc * 128:(oc + 1) * 128, :], qstage[par][:]).then_inc(
                ctx.st[par], 16))
            n["st"][par] += 16

        for i in range(n_oc_k):
            emit_item(i, *ocs[i])

        v_mm = {}
        v_ev = {}
        for tc in range(n_tc):
            par = tc % 2
            waits = [(lambda g, c=CBV: g.wait_ge(ctx.cb, c[0]))]
            if tc >= 2:
                waits.append(W(ctx.vv, v_ev[tc - 2]))
            else:
                pass

            def mmv(t, tc=tc, par=par):
                for kc in range(KC):
                    for half in range(2):
                        ins = t.matmul(psv[par][:, ts(half, TG)],
                                       x3[:, kc, ts(tc, 128)],
                                       wv3[:, kc, ts(half, TG)],
                                       start=(kc == 0), stop=(kc == KC - 1))
                ins.then_inc(ctx.pe, 1)
            v_mm[tc] = ctx.pe_op(mmv, *waits)
            sg = [W(ctx.st[par], n["st"][par])] if tc >= 2 else []
            v_ev[tc] = ctx.dve(lambda v, par=par: v.tensor_copy(
                vstage[par][:], psv[par][:]).then_inc(ctx.vv, 1),
                W(ctx.pe, v_mm[tc]), *sg)
            ctx.Qs(W(ctx.vv, v_ev[tc]))
            ctx.Qs(lambda sy, tc=tc, par=par: sy.dma_start(
                cv[tc * 128:(tc + 1) * 128, :], vstage[par][:]).then_inc(
                ctx.st[par], 16))
            n["st"][par] += 16

        kv0, kv1 = n["st"][0], n["st"][1]
        ctx.Qs(lambda sy: sy.wait_ge(ctx.st[0], kv0))
        ctx.Qs(lambda sy: sy.wait_ge(ctx.st[1], kv1))
        ctx.Qs(lambda sy: sy.sem_inc(ctx.cb, 1))
        n["cb"] += 1
        cbkv = n["cb"]
        if ag is not None:
            ag(cbkv)

        for i in range(n_oc_k, n_oc_k + n_oc_q):
            emit_item(i, *ocs[i])

        qt0, qt1 = n["st"][0], n["st"][1]
        ctx.Qs(lambda sy: sy.wait_ge(ctx.st[0], qt0))
        ctx.Qs(lambda sy: sy.wait_ge(ctx.st[1], qt1))
        ctx.barrier()


def emit_allgather(ctx, ck, cv, gk, gv, cbkv=None, barrier=True):
    if cbkv is not None:
        ctx.Qg(W(ctx.cb, cbkv))
    ctx.Qg(lambda g: g.collective_compute(
        "AllGather", ALU.bypass,
        replica_groups=[[0, 1], [2, 3], [4, 5], [6, 7]],
        ins=[ck.ap().opt()], outs=[gk.ap().opt()]).then_inc(ctx.cs))
    ctx.Qg(lambda g: g.collective_compute(
        "AllGather", ALU.bypass,
        replica_groups=[[0, 1], [2, 3], [4, 5], [6, 7]],
        ins=[cv.ap().opt()], outs=[gv.ap().opt()]).then_inc(ctx.cs))
    ctx.n["cs"] += 2
    tgt = ctx.n["cs"]
    if barrier:
        ctx.Qg(lambda g: g.wait_ge(ctx.cs, tgt))
        ctx.barrier()
    return tgt


def emit_attn_phase(ctx, cst, layer, q_dr, gk, gv, m_ap, isl_ap, at_dr,
                    cs_wait=None):
    nc, n = ctx.nc, ctx.n
    _u = 'u%d' % n.setdefault('uid', 0); n['uid'] += 1
    NH = 32
    n_qg = 2
    NM = 8 if layer == 0 else 16
    with ExitStack() as px:
        pe_ = px.enter_context
        q_sb = pe_(nc.sbuf_tensor(_u + "at_q", [128, NH * TPC], BF16))
        k_sb = pe_(nc.sbuf_tensor(_u + "at_k", [128, NKV * S], BF16))
        v_sb = pe_(nc.sbuf_tensor(_u + "at_v", [128, NKV * S], BF16))
        m_sb = pe_(nc.sbuf_tensor(_u + "at_m", [128, NM * TG], BF16))
        isl = pe_(nc.sbuf_tensor(_u + "at_isl", [128, NH * 128], BF16)) \
            if layer == 1 else None
        sqb = [pe_(nc.sbuf_tensor(_u + f"at_sq{i}", [128, S], BF16)) for i in range(2)]
        pr_sb = [pe_(nc.sbuf_tensor(_u + f"at_pr{i}", [128, TG], BF16)) for i in range(2)]
        negc = pe_(nc.sbuf_tensor(_u + "at_negc", [128, NH], F32))
        stat = pe_(nc.sbuf_tensor(_u + "at_stat", [1, 160], F32))
        kmax = pe_(nc.sbuf_tensor(_u + "at_kmax", [1, NKV], F32))
        rrow = pe_(nc.sbuf_tensor(_u + "at_rrow", [1, TG], F32))
        rb_sb = pe_(nc.sbuf_tensor(_u + "at_rb", [128, TG], F32))
        ob = [pe_(nc.sbuf_tensor(_u + f"at_ob{i}", [128, TG], BF16)) for i in range(2)]

        ps_s = [pe_(nc.psum_tensor(_u + f"at_ss{i}", [128, TG], F32)) for i in range(2)]
        ps_o = [pe_(nc.psum_tensor(_u + f"at_so{i}", [128, TG], F32)) for i in range(2)]
        ps_r = [pe_(nc.psum_tensor(_u + f"at_sr{i}", [128, TG], F32)) for i in range(2)]
        ps_b = pe_(nc.psum_tensor(_u + "at_sb", [128, TG], F32))

        q3 = q_sb[:].rearrange("p (h t) -> p h t", h=NH)
        k4 = k_sb[:].rearrange("p (r kv x) -> p kv r x", r=2, kv=NKV)
        k5 = k_sb[:].rearrange("p (r kv lb o) -> p kv lb r o", r=2, kv=NKV, lb=8)
        v5 = v_sb[:].rearrange("p (r lb kv d) -> p kv lb r d", r=2, lb=8, kv=NKV)
        m3 = m_sb[:].rearrange("p (i t) -> p i t", i=NM)

        ctx.Qs(lambda sy: sy.dma_start(
            q3, q_dr.rearrange("(h p) t -> p h t", p=128)).then_inc(ctx.ls, 16))
        k_r = k_sb[:].rearrange("p (r kv x) -> p r kv x", r=2, kv=NKV)
        v_r = v_sb[:].rearrange("p (r lb y) -> p r lb y", r=2, lb=8)
        if cs_wait is not None:
            ctx.Qs(W(ctx.cs, cs_wait))
        for r in range(2):
            ctx.Qs(lambda sy, r=r: sy.dma_start(
                k_r[:, r, :, :],
                gk[r * 1024:(r + 1) * 1024, :].rearrange(
                    "(kv p) x -> p kv x", p=128)).then_inc(ctx.ls, 16))
            ctx.Qs(lambda sy, r=r: sy.dma_start(
                v_r[:, r, :, :],
                gv[r * 1024:(r + 1) * 1024, :].rearrange(
                    "(lb p) y -> p lb y", p=128)).then_inc(ctx.ls, 16))
        ctx.Qs(lambda sy: sy.dma_start(m3, m_ap).then_inc(ctx.ls, 16))
        n["ls"] += 96
        if layer == 1:
            ctx.Qs(lambda sy: sy.dma_start(
                isl[:].rearrange("p (h o) -> p h o", h=NH),
                isl_ap).then_inc(ctx.ls, 16))
            n["ls"] += 16
        LSA = n["ls"]

        units = [("k", h, 4) for h in range(NKV)] + [("q", h, 2) for h in range(NH)]
        slot = {}
        s_i = 0
        for kind, h, nsl in units:
            slot[(kind, h)] = s_i
            s_i += nsl

        last_mm = {}
        last_rd = [0, 0]
        aq = {}
        for ui, (kind, h, nsl) in enumerate(units):
            par = ui % 2
            waits = [W(ctx.ls, LSA)]
            if ui >= 2:
                waits.append(W(ctx.pe, last_mm[ui - 2]))

            def sq(s, kind=kind, h=h, par=par, nsl=nsl):
                src = k4[:, h, :] if kind == "k" else q3[:, h, :]
                s.activation(sqb[par][:, 0:nsl * TG], src,
                             AF.Square).then_inc(ctx.aa, 1)
            aq[ui] = ctx.act(sq, *waits)

            for qt in range(nsl):
                sidx = slot[(kind, h)] + qt
                g = [W(ctx.aa, aq[ui])]
                if last_rd[qt % 2]:
                    g.append(W(ctx.vv, last_rd[qt % 2]))
                pid = ctx.pe_op(lambda t, qt=qt, par=par: t.matmul(
                    ps_r[qt % 2][0:1, :], cst["ones_bf"][:], sqb[par][:, ts(qt, TG)],
                    start=True, stop=True).then_inc(ctx.pe, 1), *g)
                last_rd[qt % 2] = ctx.dve(lambda v, sidx=sidx, qt=qt: v.reduce_max(
                    stat[:, sidx:sidx + 1], ps_r[qt % 2][0:1, :],
                    axis=AX.X).then_inc(ctx.vv, 1), W(ctx.pe, pid))
            last_mm[ui] = n["pe"]

        ctx.Qv(W(ctx.vv, max(last_rd)))
        im = 0
        for h in range(NKV):
            s0 = slot[("k", h)]
            im = ctx.dve(lambda v, h=h, s0=s0: v.reduce_max(
                kmax[:, h:h + 1], stat[:, s0:s0 + 4], axis=AX.X).then_inc(ctx.vv, 1))
        for h in range(NH):
            s0 = slot[("q", h)]
            im = ctx.dve(lambda v, h=h, s0=s0: v.reduce_max(
                stat[:, 128 + h:129 + h], stat[:, s0:s0 + 2],
                axis=AX.X).then_inc(ctx.vv, 1))
        ctx.Qv(W(ctx.vv, im))
        for h in range(NH):
            kv = h // G
            im = ctx.dve(lambda v, h=h, kv=kv: v.tensor_tensor(
                stat[:, 128 + h:129 + h], stat[:, 128 + h:129 + h],
                kmax[:, kv:kv + 1], ALU.mult).then_inc(ctx.vv, 1))
        ctx.Qv(W(ctx.vv, im))
        isq = ctx.act(lambda s: s.activation(
            stat[:, 128:128 + NH], stat[:, 128:128 + NH], AF.Sqrt,
            scale=SCALE * SCALE).then_inc(ctx.aa, 1), W(ctx.vv, im))
        ibc = ctx.pe_op(lambda t: t.matmul(
            ps_b[:, 0:NH], cst["ones_rf"][:], stat[:, 128:128 + NH],
            start=True, stop=True).then_inc(ctx.pe, 1), W(ctx.aa, isq))
        inc_ = ctx.dve(lambda v: v.tensor_scalar(
            negc[:], ps_b[:, 0:NH], -1.0, None, ALU.mult).then_inc(ctx.vv, 1),
            W(ctx.pe, ibc))

        items = []
        tiles = []
        for h in range(NH):
            for qg in range(n_qg):
                kcs = list(range(8 * qg + 8))
                tl = []
                for kc in kcs:
                    tl.append(len(tiles))
                    tiles.append(dict(h=h, qg=qg, kc=kc,
                                      msk=(layer == 1) or (kc >= 8 * qg)))
                items.append(dict(h=h, qg=qg, kcs=kcs, tiles=tl))

        pe_sc = {}; pe_sum = {}; pe_bc = {}; ex_c = {}
        dv_rec = {}; dv_rb = {}; dv_div = {}
        c_pe = n["pe"]
        c_ex = n.get("ex", 0)
        c_dv = n["vv"]
        for si, it in enumerate(items):
            nt = len(it["tiles"])
            ops = []
            for j in range(nt):
                ops.append(("s", j))
                if j >= 2:
                    ops.append(("a", j - 2))
            ops += [("a", nt - 2), ("a", nt - 1)]
            it["ops"] = ops
            for kind, j in ops:
                t_ = it["tiles"][j]
                c_pe += 1
                if kind == "s":
                    pe_sc[t_] = c_pe
                else:
                    pe_sum[t_] = c_pe
            for t_ in it["tiles"]:
                c_ex += 1
                ex_c[t_] = c_ex
            dv_rec[si] = c_dv + 1
            dv_div[si] = c_dv + 2
            c_dv += 2

        st_base = [n["st"][0], n["st"][1]]

        def emit_pe(t):
            t.wait_ge(ctx.vv, inc_)
            for si, it in enumerate(items):
                h, qg, kcs, tls = it["h"], it["qg"], it["kcs"], it["tiles"]
                kv = h // G
                nt = len(kcs)
                for kind, j in it["ops"]:
                    kc, t_ = kcs[j], tls[j]
                    if kind == "s":
                        prev = t_ - 2
                        if prev >= 0:
                            t.wait_ge(ctx.ex, ex_c[prev])
                        msk = tiles[t_]["msk"]
                        ins = t.matmul(ps_s[t_ % 2][:],
                                       k5[:, kv, kc // 2, kc % 2, :],
                                       q3[:, h, ts(qg, TG)],
                                       start=True, stop=not msk)
                        if msk:
                            if layer == 0:
                                ins = t.matmul(ps_s[t_ % 2][:], cst["ident_bf"][:],
                                               m3[:, kc - 8 * qg, :],
                                               start=False, stop=True)
                            else:
                                dl = 8 * qg - kc + 7
                                ins = t.matmul(ps_s[t_ % 2][:],
                                               isl[:, h * 128:(h + 1) * 128],
                                               m3[:, dl, :], start=False, stop=True)
                        ins.then_inc(ctx.pe, 1)
                    else:
                        t.wait_ge(ctx.ex, ex_c[t_])
                        if j == 0 and si >= 2:
                            t.wait_ge(ctx.vv, dv_div[si - 2])
                        t.matmul(ps_o[si % 2][:], v5[:, kv, kc // 2, kc % 2, :],
                                 pr_sb[t_ % 2][:], start=(j == 0),
                                 stop=(j == nt - 1))
                        t.matmul(ps_r[si % 2][:], cst["ones_bb"][:],
                                 pr_sb[t_ % 2][:], start=(j == 0),
                                 stop=(j == nt - 1)).then_inc(ctx.pe, 1)
        ctx.Qt(emit_pe)

        def emit_act(s):
            s.wait_ge(ctx.vv, inc_)
            for si, it in enumerate(items):
                h = it["h"]
                for j, t_ in enumerate(it["tiles"]):
                    prev = t_ - 2
                    if prev >= 0:
                        s.wait_ge(ctx.pe, pe_sum[prev])
                    s.wait_ge(ctx.pe, pe_sc[t_])
                    s.activation(pr_sb[t_ % 2][:], ps_s[t_ % 2][:], AF.Exp,
                                 bias=negc[:, h:h + 1],
                                 scale=SCALE).then_inc(ctx.ex, 1)
        ctx.Qa(emit_act)

        def emit_dve(v):
            v.wait_ge(ctx.vv, inc_)
            for si, it in enumerate(items):
                par = si % 2
                v.wait_ge(ctx.pe, pe_sum[it["tiles"][-1]])
                if si >= 1:
                    v.wait_ge(ctx.vv, dv_div[si - 1])
                v.reciprocal(rb_sb[:], ps_r[par][:]).then_inc(ctx.vv, 1)
                v.wait_ge(ctx.vv, dv_rec[si])
                if si >= 2:
                    v.wait_ge(ctx.st[par], st_base[par] + 16 * (si // 2))
                v.tensor_tensor(ob[par][:], ps_o[par][:], rb_sb[:],
                                ALU.mult).then_inc(ctx.vv, 1)
        ctx.Qv(emit_dve)

        def emit_st(sy):
            for si, it in enumerate(items):
                sy.wait_ge(ctx.vv, dv_div[si])
                sy.dma_start(at_dr[it["h"] * 128:(it["h"] + 1) * 128,
                                   ts(it["qg"], TG)],
                             ob[si % 2][:]).then_inc(ctx.st[si % 2], 16)
        ctx.Qs(emit_st)

        n["vv"] = c_dv
        n["pe"] = c_pe
        n["ex"] = c_ex
        n["st"][0] += 16 * ((len(items) + 1) // 2)
        n["st"][1] += 16 * (len(items) // 2)
        t0, t1 = n["st"][0], n["st"][1]
        ctx.Qs(lambda sy: sy.wait_ge(ctx.st[0], t0))
        ctx.Qs(lambda sy: sy.wait_ge(ctx.st[1], t1))
        ctx.barrier()


def emit_postattn_phase(ctx, cst, layer, attn_ap, xt_ap, wo_ap, extra, outs):
    nc, n = ctx.nc, ctx.n
    _u = 'u%d' % n.setdefault('uid', 0); n['uid'] += 1
    EPS = 1e-5
    if layer == 0:
        w2_i, wsc_i, bias_i = extra
        out_o, = outs
    else:
        x1_o, x1b_o = outs

    with ExitStack() as px:
        pe_ = px.enter_context
        y3b = pe_(nc.sbuf_tensor(_u + "pa_y3", [128, KC * TG], F32))
        ab = pe_(nc.sbuf_tensor(_u + "pa_at", [128, KC * TG], BF16))
        wob = [pe_(nc.sbuf_tensor(_u + f"pa_wo{i}", [128, KC * 128], BF16)) for i in range(2)]
        xtile = [pe_(nc.sbuf_tensor(_u + f"pa_xt{i}", [128, TG], F32)) for i in range(2)]
        ybf = [pe_(nc.sbuf_tensor(_u + f"pa_yb{i}", [128, TG], BF16)) for i in range(2)]
        ysq = [pe_(nc.sbuf_tensor(_u + f"pa_yq{i}", [128, TG], BF16)) for i in range(2)]
        tno = [pe_(nc.sbuf_tensor(_u + f"pa_tn{i}", [128, TG], F32)) for i in range(2)]
        ostg = [pe_(nc.sbuf_tensor(_u + f"pa_os{i}", [128, TG], F32)) for i in range(2)]
        rows = pe_(nc.sbuf_tensor(_u + "pa_rows", [1, 8 * TG], F32))
        mb_sb = pe_(nc.sbuf_tensor(_u + "pa_mb", [128, TG], F32))
        rb_sb = pe_(nc.sbuf_tensor(_u + "pa_rb", [128, TG], F32))
        if layer == 0:
            xnb = pe_(nc.sbuf_tensor(_u + "pa_xn", [128, KC * TG], BF16))
            aab = [pe_(nc.sbuf_tensor(_u + f"pa_aa{i}", [128, TG], F32)) for i in range(2)]
            w2b = [pe_(nc.sbuf_tensor(_u + f"pa_w2{i}", [128, KC * 128], BF16)) for i in range(2)]
            amx = pe_(nc.sbuf_tensor(_u + "pa_amx", [128, TG], F32))
            tmax = pe_(nc.sbuf_tensor(_u + "pa_tmax", [128, 4], F32))
            bias_sb = pe_(nc.sbuf_tensor(_u + "pa_bias", [128, KC], F32))
            wsc_sb = pe_(nc.sbuf_tensor(_u + "pa_wsc", [1, 1], F32))
            sb_sb = pe_(nc.sbuf_tensor(_u + "pa_sb", [128, TG], F32))
            rsneg = pe_(nc.sbuf_tensor(_u + "pa_rsn", [1, TG], BF16))
        else:
            x1bstg = [pe_(nc.sbuf_tensor(_u + f"pa_xb{i}", [128, TG], BF16)) for i in range(2)]

        psp = [pe_(nc.psum_tensor(_u + f"pa_ps{i}", [128, TG], F32)) for i in range(2)]
        pss = pe_(nc.psum_tensor(_u + "pa_pss", [1, TG], F32))
        psq = pe_(nc.psum_tensor(_u + "pa_psq", [1, TG], F32))
        psb = [pe_(nc.psum_tensor(_u + f"pa_pb{i}", [128, TG], F32)) for i in range(2)]
        psr = pe_(nc.psum_tensor(_u + "pa_pr", [1, 128], F32))

        y33 = y3b[:].rearrange("p (kc t) -> p kc t", kc=KC)
        ab3 = ab[:].rearrange("p (kc t) -> p kc t", kc=KC)
        if layer == 0:
            xn3 = xnb[:].rearrange("p (kc t) -> p kc t", kc=KC)
        R_M, R_V, R_S, R_R, R_AB, R_SE, R_RN, R_MS = range(8)

        def row(i):
            return rows[:, i * TG:(i + 1) * TG]

        dve, act, pe_op = ctx.dve, ctx.act, ctx.pe_op
        Qg, Qs = ctx.Qg, ctx.Qs
        vv, aa, pe, ex = ctx.vv, ctx.aa, ctx.pe, ctx.ex
        lg, ls, st, cb = ctx.lg, ctx.ls, ctx.st, ctx.cb
        wg, lx = ctx.wg, ctx.lx

        if layer == 0:
            Qg(lambda g: g.dma_start(wsc_sb[:], wsc_i).then_inc(lg, 16))
            n["lg"] += 16
            for oc in range(KC):
                Qg(lambda g, oc=oc: g.dma_start(
                    bias_sb[:, oc:oc + 1],
                    bias_i[oc * 128:(oc + 1) * 128].rearrange("(p o) -> p o", o=1),
                ).then_inc(lg, 16))
                n["lg"] += 16
        LB = n["lg"]
        CB = n["cb"]

        prev_projmm = {}; prev_add = {}; prev_stats = {}
        prev_m1 = {}; prev_ffnmm = {}
        id_var = [0]
        st_base = [n["st"][0], n["st"][1]]

        for hf in range(2):
            if hf == 1:
                Qs(W(pe, prev_projmm[(0, KC - 1)]))
            Qs(lambda sy, hf=hf: sy.dma_start(
                ab3, attn_ap[:, hf * TG:(hf + 1) * TG].rearrange(
                    "(kc p) t -> p kc t", p=128)).then_inc(ls, 16))
            n["ls"] += 16
            LA = n["ls"]

            for oc in range(KC):
                par = oc % 2
                first2 = (hf, oc) in [(0, 0), (0, 1)]
                if not first2:
                    poc, phf = oc - 2, hf
                    if poc < 0:
                        poc += KC; phf -= 1
                if ctx.wgl[par]:
                    Qg(W(pe, ctx.wgl[par]))
                Qg(lambda g, oc=oc, par=par: g.dma_start(
                    wob[par][:].rearrange("p (kc o) -> p kc o", kc=KC),
                    wo_ap[:, oc * 128:(oc + 1) * 128].rearrange(
                        "(kc p) o -> p kc o", p=128)).then_inc(wg[par], 16))
                n["wg"][par] += 16
                if not first2:
                    Qs(W(vv, prev_add[(phf, poc)]))
                Qs(lambda sy, oc=oc, par=par, hf=hf: sy.dma_start(
                    xtile[par][:],
                    xt_ap[oc * 128:(oc + 1) * 128, hf * TG:(hf + 1) * TG],
                ).then_inc(lx[par], 16))
                n["lx"][par] += 16

                waits = [W(wg[par], n["wg"][par]), W(ls, LA)]
                if not first2:
                    waits.append(W(vv, prev_add[(phf, poc)]))

                def projmm(t, par=par):
                    w3 = wob[par][:].rearrange("p (kc o) -> p kc o", kc=KC)
                    for kc in range(KC):
                        ins = t.matmul(psp[par][:], w3[:, kc, :], ab3[:, kc, :],
                                       start=(kc == 0), stop=(kc == KC - 1))
                    ins.then_inc(pe, 1)
                prev_projmm[(hf, oc)] = pe_op(projmm, *waits)
                ctx.wgl[par] = prev_projmm[(hf, oc)]

                waits = [W(pe, prev_projmm[(hf, oc)]), W(lx[par], n["lx"][par])]
                if hf == 1:
                    waits += [W(st[0], n["st"][0]), W(st[1], n["st"][1])]
                ia = dve(lambda v, oc=oc, par=par: v.tensor_tensor(
                    y33[:, oc, :], psp[par][:], xtile[par][:],
                    ALU.add).then_inc(vv, 1), *waits)
                prev_add[(hf, oc)] = ia
                waits = [W(vv, ia)]
                if not first2:
                    waits += [W(pe, prev_stats[(phf, poc)])]
                ib = dve(lambda v, oc=oc, par=par: v.tensor_copy(
                    ybf[par][:], y33[:, oc, :]).then_inc(vv, 1), *waits)
                iq = act(lambda s, par=par: s.activation(
                    ysq[par][:], ybf[par][:], AF.Square).then_inc(aa, 1),
                    W(vv, ib))
                waits = [W(vv, ib), W(aa, iq), W(lg, LB), W(cb, CB)]
                if hf == 1 and oc in (0, 1):
                    waits.append(W(vv, id_var[0]))

                def statsmm(t, oc=oc, par=par):
                    t.matmul(pss[:], cst["ones_bf"][:], ybf[par][:],
                             start=(oc == 0), stop=(oc == KC - 1))
                    t.matmul(psq[:], cst["ones_bf"][:], ysq[par][:],
                             start=(oc == 0), stop=(oc == KC - 1)).then_inc(pe, 1)
                prev_stats[(hf, oc)] = pe_op(statsmm, *waits)

            def rows_pass(wait_cnt):
                im = dve(lambda v: v.tensor_scalar(row(R_M), pss[:], 1.0 / D, None,
                                                   ALU.mult).then_inc(vv, 1),
                         W(pe, wait_cnt))
                i2 = dve(lambda v: v.tensor_tensor(row(R_V), row(R_M), row(R_M),
                                                   ALU.mult).then_inc(vv, 1),
                         W(vv, im))
                iv = dve(lambda v: v.scalar_tensor_tensor(
                    row(R_V), psq[:], 1.0 / D, row(R_V), ALU.mult,
                    ALU.subtract).then_inc(vv, 1), W(vv, i2))
                isd = act(lambda s: s.activation(
                    row(R_S), row(R_V), AF.Sqrt,
                    bias=cst["eps"][0:1, 0:1]).then_inc(aa, 1), W(vv, iv))
                ir = dve(lambda v: v.reciprocal(row(R_R), row(R_S)).then_inc(vv, 1),
                         W(aa, isd))
                ibc = pe_op(lambda t: (
                    t.matmul(psb[0][:], cst["ones_rf"][:], row(R_M), start=True,
                             stop=True),
                    t.matmul(psb[1][:], cst["ones_rf"][:], row(R_R), start=True,
                             stop=True).then_inc(pe, 1)), W(vv, ir))
                imb = dve(lambda v: v.tensor_copy(mb_sb[:], psb[0][:]).then_inc(vv, 1),
                          W(pe, ibc))
                irb = dve(lambda v: v.tensor_copy(rb_sb[:], psb[1][:]).then_inc(vv, 1))
                return iv, irb

            iv_, irb = rows_pass(prev_stats[(hf, KC - 1)])
            id_var[0] = iv_

            id_x1 = {}
            prev_amx = 0
            for oc in range(KC):
                par = oc % 2
                i1 = dve(lambda v, oc=oc, par=par: v.tensor_tensor(
                    tno[par][:], y33[:, oc, :], mb_sb[:],
                    ALU.subtract).then_inc(vv, 1), W(vv, irb))
                i2_ = dve(lambda v, oc=oc, par=par: v.tensor_tensor(
                    y33[:, oc, :], tno[par][:], rb_sb[:],
                    ALU.mult).then_inc(vv, 1), W(vv, i1))
                id_x1[oc] = i2_
                if layer == 0:
                    ag = [W(vv, i2_)]
                    if oc >= 2:
                        ag.append(W(vv, prev_amx))
                    iab = act(lambda s, oc=oc, par=par: s.activation(
                        aab[par][:], y33[:, oc, :], AF.Abs).then_inc(aa, 1), *ag)
                    if oc == 0:
                        prev_amx = dve(lambda v, par=par: v.tensor_copy(
                            amx[:], aab[par][:]).then_inc(vv, 1), W(aa, iab))
                    else:
                        prev_amx = dve(lambda v, par=par: v.tensor_tensor(
                            amx[:], amx[:], aab[par][:], ALU.max).then_inc(vv, 1),
                            W(aa, iab), W(vv, prev_amx))
                else:
                    sg = [W(st[par], n["st"][par])] if n["st"][par] > st_base[par] else []
                    ix = dve(lambda v, oc=oc, par=par: v.tensor_copy(
                        x1bstg[par][:], y33[:, oc, :]).then_inc(vv, 1),
                        W(vv, i2_), *sg)
                    Qs(W(vv, ix))
                    Qs(lambda sy, oc=oc, par=par, hf=hf: sy.dma_start(
                        x1b_o[oc * 128:(oc + 1) * 128, hf * TG:(hf + 1) * TG],
                        x1bstg[par][:]).then_inc(st[par], 16))
                    n["st"][par] += 16
                    Qs(lambda sy, oc=oc, par=par, hf=hf: sy.dma_start(
                        x1_o[oc * 128:(oc + 1) * 128, hf * TG:(hf + 1) * TG],
                        y33[:, oc, :]).then_inc(st[par], 16))
                    n["st"][par] += 16

            if layer == 1:
                continue

            id_cp = {}
            for tr in range(4):
                par = tr % 2
                itp = pe_op(lambda t, tr=tr, par=par: t.transpose(
                    psb[par][:, 0:128], amx[:, tr * 128:(tr + 1) * 128],
                    cst["ident_f"][:]).then_inc(pe, 1),
                    W(vv, prev_amx), W(lg, LB), W(cb, CB),
                    *([W(vv, id_cp[tr - 2])] if tr >= 2 else []))
                id_cp[tr] = dve(lambda v, tr=tr, par=par: v.reduce_max(
                    tmax[:, tr:tr + 1], psb[par][:, 0:128],
                    axis=AX.X).then_inc(vv, 1), W(pe, itp))
            id_rr = {}
            for tr in range(4):
                itp = pe_op(lambda t, tr=tr: t.transpose(
                    psr[:], tmax[:, tr:tr + 1], cst["ident_f"][:]).then_inc(pe, 1),
                    W(vv, id_cp[3]),
                    *([W(vv, id_rr[tr - 1])] if tr >= 1 else []))
                id_rr[tr] = dve(lambda v, tr=tr: v.tensor_copy(
                    row(R_AB)[:, tr * 128:(tr + 1) * 128],
                    psr[:]).then_inc(vv, 1), W(pe, itp))
            icl = dve(lambda v: v.tensor_scalar_max(row(R_AB), row(R_AB),
                                                    EPS).then_inc(vv, 1),
                      W(vv, id_rr[3]))
            ise = dve(lambda v: v.tensor_scalar(row(R_SE), row(R_AB),
                                                wsc_sb[0:1, 0:1], None,
                                                ALU.mult).then_inc(vv, 1),
                      W(vv, icl))
            irn = dve(lambda v: v.reciprocal(row(R_RN), row(R_AB)).then_inc(vv, 1),
                      W(vv, ise))
            ibc2 = pe_op(lambda t: t.matmul(psb[0][:], cst["ones_rf"][:], row(R_RN),
                                            start=True, stop=True).then_inc(pe, 1),
                         W(vv, irn))
            irnb = dve(lambda v: v.tensor_copy(mb_sb[:], psb[0][:]).then_inc(vv, 1),
                       W(pe, ibc2))
            id_xn = {}
            for oc in range(KC):
                id_xn[oc] = dve(lambda v, oc=oc: v.tensor_tensor(
                    xn3[:, oc, :], y33[:, oc, :], mb_sb[:],
                    ALU.mult).then_inc(vv, 1), W(vv, irnb))

            irs = pe_op(lambda t: [
                t.matmul(pss[:], cst["ones_bf"][:], xn3[:, kc, :], start=(kc == 0),
                         stop=(kc == KC - 1)) for kc in range(KC)][-1].then_inc(pe, 1),
                *[W(vv, id_xn[kc]) for kc in range(KC)])
            irsn = dve(lambda v: v.tensor_scalar(rsneg[:], pss[:], -1.5, None,
                                                 ALU.mult).then_inc(vv, 1),
                       W(pe, irs))
            ibc3 = pe_op(lambda t: t.matmul(psb[1][:], cst["ones_rf"][:], row(R_SE),
                                            start=True, stop=True).then_inc(pe, 1),
                         W(vv, irsn))
            isb = dve(lambda v: v.tensor_copy(sb_sb[:], psb[1][:]).then_inc(vv, 1),
                      W(pe, ibc3))

            for oc in range(KC):
                par = oc % 2
                first2 = (hf, oc) in [(0, 0), (0, 1)]
                if ctx.wgl[par]:
                    Qg(W(pe, ctx.wgl[par]))
                Qg(lambda g, oc=oc, par=par: g.dma_start(
                    w2b[par][:].rearrange("p (kc o) -> p kc o", kc=KC),
                    w2_i[:, oc * 128:(oc + 1) * 128].rearrange(
                        "(kc p) o -> p kc o", p=128)).then_inc(wg[par], 16))
                n["wg"][par] += 16

                waits = [W(wg[par], n["wg"][par]), W(vv, isb)]
                if not first2:
                    poc, phf = oc - 2, hf
                    if poc < 0:
                        poc += KC; phf -= 1
                    waits.append(W(vv, prev_m1[(phf, poc)]))

                def ffnmm(t, par=par):
                    w3 = w2b[par][:].rearrange("p (kc o) -> p kc o", kc=KC)
                    for kc in range(KC):
                        t.matmul(psp[par][:], w3[:, kc, :], xn3[:, kc, :],
                                 start=(kc == 0), stop=False)
                    t.matmul(psp[par][:], cst["ones_rb"][:], rsneg[:],
                             start=False, stop=True).then_inc(pe, 1)
                prev_ffnmm[(hf, oc)] = pe_op(ffnmm, *waits)
                ctx.wgl[par] = prev_ffnmm[(hf, oc)]

                im1 = dve(lambda v, par=par: v.tensor_tensor(
                    tno[par][:], psp[par][:], sb_sb[:], ALU.mult).then_inc(vv, 1),
                    W(pe, prev_ffnmm[(hf, oc)]))
                prev_m1[(hf, oc)] = im1
                iy2 = dve(lambda v, oc=oc, par=par: v.scalar_tensor_tensor(
                    y33[:, oc, :], tno[par][:], bias_sb[:, oc:oc + 1],
                    y33[:, oc, :], ALU.add, ALU.add).then_inc(vv, 1), W(vv, im1))
                waits = [W(vv, iy2)]
                if not first2:
                    waits += [W(pe, prev_stats[("f", phf, poc)])]
                ib = dve(lambda v, oc=oc, par=par: v.tensor_copy(
                    ybf[par][:], y33[:, oc, :]).then_inc(vv, 1), *waits)
                iq = act(lambda s, par=par: s.activation(
                    ysq[par][:], ybf[par][:], AF.Square).then_inc(aa, 1), W(vv, ib))
                waits = [W(vv, ib), W(aa, iq)]
                if oc in (0, 1):
                    waits.append(W(vv, id_var[0]))

                def statsmm(t, oc=oc, par=par):
                    t.matmul(pss[:], cst["ones_bf"][:], ybf[par][:],
                             start=(oc == 0), stop=(oc == KC - 1))
                    t.matmul(psq[:], cst["ones_bf"][:], ysq[par][:],
                             start=(oc == 0), stop=(oc == KC - 1)).then_inc(pe, 1)
                prev_stats[("f", hf, oc)] = pe_op(statsmm, *waits)

            iv_, irb = rows_pass(prev_stats[("f", hf, KC - 1)])
            id_var[0] = iv_

            for oc in range(KC):
                par = oc % 2
                i1 = dve(lambda v, oc=oc, par=par: v.tensor_tensor(
                    tno[par][:], y33[:, oc, :], mb_sb[:],
                    ALU.subtract).then_inc(vv, 1), W(vv, irb))
                sg = [W(st[par], n["st"][par])] if n["st"][par] > st_base[par] else []
                i2_ = dve(lambda v, oc=oc, par=par: v.tensor_tensor(
                    ostg[par][:], tno[par][:], rb_sb[:],
                    ALU.mult).then_inc(vv, 1), W(vv, i1), *sg)
                Qs(W(vv, i2_))
                Qs(lambda sy, oc=oc, par=par, hf=hf: sy.dma_start(
                    out_o[oc * 128:(oc + 1) * 128, hf * TG:(hf + 1) * TG],
                    ostg[par][:]).then_inc(st[par], 16))
                n["st"][par] += 16

        t0, t1 = n["st"][0], n["st"][1]
        ctx.Qs(lambda sy: sy.wait_ge(st[0], t0))
        ctx.Qs(lambda sy: sy.wait_ge(st[1], t1))
        ctx.barrier()


def emit_moe_phase(ctx, cst, x3_ap, x3b_ap, gate_i, up_i, down_i, fc1_i, fc2_i,
                   exp_i, gam_i, bet_i, h_dr, out_ap):
    nc, n = ctx.nc, ctx.n
    _u = 'u%d' % n.setdefault('uid', 0); n['uid'] += 1
    INT, LAT, NE = 8192, 2048, 8
    KCL = LAT // 128
    EPS = 1e-5
    dve, act, pe_op = ctx.dve, ctx.act, ctx.pe_op
    Qg, Qs = ctx.Qg, ctx.Qs
    vv, aa, pe, ex = ctx.vv, ctx.aa, ctx.pe, ctx.ex
    lg, ls, st, cb = ctx.lg, ctx.ls, ctx.st, ctx.cb
    wg, lx = ctx.wg, ctx.lx

    with ExitStack() as px:
        pe_ = px.enter_context
        rout = pe_(nc.sbuf_tensor(_u + "mo_rout", [128, KCL * TPC], BF16))
        wbA = [pe_(nc.sbuf_tensor(_u + f"mo_wA{i}", [128, KC * 128], BF16)) for i in range(2)]
        gate_sb = pe_(nc.sbuf_tensor(_u + "mo_gate", [128, KC * NE], BF16))
        tno = [pe_(nc.sbuf_tensor(_u + f"mo_tn{i}", [128, TG], F32)) for i in range(2)]
        rows = pe_(nc.sbuf_tensor(_u + "mo_rows", [1, 4 * TG], F32))
        mb_sb = pe_(nc.sbuf_tensor(_u + "mo_mb", [128, TPC], F32))
        rb_sb = pe_(nc.sbuf_tensor(_u + "mo_rb", [128, TPC], F32))
        mb2_sb = pe_(nc.sbuf_tensor(_u + "mo_mb2", [128, TPC], F32))
        rb2_sb = pe_(nc.sbuf_tensor(_u + "mo_rb2", [128, TPC], F32))
        gam_sb = pe_(nc.sbuf_tensor(_u + "mo_gam", [128, KC], F32))
        bet_sb = pe_(nc.sbuf_tensor(_u + "mo_bet", [128, KC], F32))

        psp = [pe_(nc.psum_tensor(_u + f"mo_ps{i}", [128, TG], F32)) for i in range(4)]
        pss = pe_(nc.psum_tensor(_u + "mo_pss", [1, 2 * TG], F32))
        psq = pe_(nc.psum_tensor(_u + "mo_psq", [1, 2 * TG], F32))

        rt3 = rout[:].rearrange("p (kc t) -> p kc t", kc=KCL)
        g3 = gate_sb[:].rearrange("p (kc o) -> p kc o", kc=KC)

        def row(i):
            return rows[:, i * TG:(i + 1) * TG]
        R_M, R_V, R_S, R_R = range(4)

        Qg(lambda g: g.dma_start(
            g3, gate_i.rearrange("(kc p) o -> p kc o", p=128)).then_inc(lg, 16))
        n["lg"] += 16
        for oc in range(KC):
            Qg(lambda g, oc=oc: g.dma_start(
                gam_sb[:, oc:oc + 1],
                gam_i[oc * 128:(oc + 1) * 128].rearrange("(p o) -> p o", o=1),
            ).then_inc(lg, 16))
            Qg(lambda g, oc=oc: g.dma_start(
                bet_sb[:, oc:oc + 1],
                bet_i[oc * 128:(oc + 1) * 128].rearrange("(p o) -> p o", o=1),
            ).then_inc(lg, 16))
            n["lg"] += 32
        LB = n["lg"]
        st_base = [n["st"][0], n["st"][1]]

        with ExitStack() as pxa:
            pa = pxa.enter_context
            x1b = pa(nc.sbuf_tensor(_u + "mo_x1b", [128, KC * TPC], BF16))
            lat = pa(nc.sbuf_tensor(_u + "mo_lat", [128, KCL * TPC], BF16))
            lg_sb = pa(nc.sbuf_tensor(_u + "mo_lg", [8, TPC], F32))
            lt_sb = pa(nc.sbuf_tensor(_u + "mo_lt", [128, NE], F32))
            top_sb = pa(nc.sbuf_tensor(_u + "mo_top", [128, 8], F32))
            eq_sb = pa(nc.sbuf_tensor(_u + "mo_eq", [128, NE], F32))
            s2row = pa(nc.sbuf_tensor(_u + "mo_s2r", [1, TPC], F32))
            s2b = pa(nc.sbuf_tensor(_u + "mo_s2b", [128, TPC], F32))
            hstg = [pa(nc.sbuf_tensor(_u + f"mo_hs{i}", [128, TPC], BF16)) for i in range(2)]
            xb3 = x1b[:].rearrange("p (kc t) -> p kc t", kc=KC)
            lt3f = lat[:].rearrange("p (kc t) -> p kc t", kc=KCL)

            Qs(lambda sy: sy.dma_start(
                xb3, x3b_ap.rearrange("(kc p) t -> p kc t", p=128)).then_inc(ls, 16))
            n["ls"] += 16
            LXB = n["ls"]

            def gmm(t):
                for tg in range(2):
                    for kc in range(KC):
                        ins = t.matmul(psp[tg][0:8, :], g3[:, kc, :],
                                       xb3[:, kc, ts(tg, TG)],
                                       start=(kc == 0), stop=(kc == KC - 1))
                ins.then_inc(pe, 1)
            ig = pe_op(gmm, W(ls, LXB), W(lg, LB), W(cb, n["cb"]))
            il0 = dve(lambda v: v.tensor_copy(lg_sb[:, 0:TG], psp[0][0:8, :]).then_inc(vv, 1),
                      W(pe, ig))
            il = dve(lambda v: v.tensor_copy(lg_sb[:, TG:TPC], psp[1][0:8, :]).then_inc(vv, 1))
            id_s2 = {}
            for qt in range(8):
                itp = pe_op(lambda t, qt=qt: t.transpose(
                    psp[2][0:128, 0:8], lg_sb[:, qt * 128:(qt + 1) * 128],
                    cst["ident_f"][0:8, 0:8]).then_inc(pe, 1),
                    W(vv, il), *([W(vv, id_s2[qt - 1])] if qt >= 1 else []))
                i0 = dve(lambda v: v.tensor_copy(lt_sb[:], psp[2][0:128, 0:8]).then_inc(vv, 1),
                         W(pe, itp))
                i1 = dve(lambda v: v.reduce_max(top_sb[:, 0:1], lt_sb[:],
                                                axis=AX.X).then_inc(vv, 1), W(vv, i0))
                i2 = dve(lambda v: v.tensor_scalar(eq_sb[:], lt_sb[:], top_sb[:, 0:1],
                                                   None, ALU.is_equal).then_inc(vv, 1),
                         W(vv, i1))
                i3 = dve(lambda v: v.reduce_sum(top_sb[:, 1:2], eq_sb[:],
                                                axis=AX.X).then_inc(vv, 1), W(vv, i2))
                i4 = dve(lambda v: v.scalar_tensor_tensor(
                    eq_sb[:], eq_sb[:], NEG, lt_sb[:], ALU.mult,
                    ALU.add).then_inc(vv, 1), W(vv, i3))
                i5 = dve(lambda v: v.reduce_max(top_sb[:, 2:3], eq_sb[:],
                                                axis=AX.X).then_inc(vv, 1), W(vv, i4))
                i6 = dve(lambda v: v.tensor_scalar(top_sb[:, 3:4], top_sb[:, 1:2],
                                                   -1.0, 1.0, ALU.add,
                                                   ALU.min).then_inc(vv, 1), W(vv, i5))
                i7 = dve(lambda v: v.tensor_tensor(top_sb[:, 4:5], top_sb[:, 0:1],
                                                   top_sb[:, 2:3],
                                                   ALU.subtract).then_inc(vv, 1),
                         W(vv, i6))
                i8 = dve(lambda v: v.tensor_tensor(top_sb[:, 5:6], top_sb[:, 3:4],
                                                   top_sb[:, 4:5],
                                                   ALU.mult).then_inc(vv, 1), W(vv, i7))
                i9 = dve(lambda v: v.tensor_tensor(top_sb[:, 6:7], top_sb[:, 0:1],
                                                   top_sb[:, 2:3],
                                                   ALU.add).then_inc(vv, 1), W(vv, i8))
                i10 = dve(lambda v: v.tensor_tensor(top_sb[:, 7:8], top_sb[:, 6:7],
                                                    top_sb[:, 5:6],
                                                    ALU.add).then_inc(vv, 1), W(vv, i9))
                itr = pe_op(lambda t: t.transpose(
                    psp[3][0:1, 0:128], top_sb[:, 7:8],
                    cst["ident_f"][:]).then_inc(pe, 1), W(vv, i10))
                id_s2[qt] = dve(lambda v, qt=qt: v.tensor_copy(
                    s2row[:, qt * 128:(qt + 1) * 128],
                    psp[3][0:1, 0:128]).then_inc(vv, 1), W(pe, itr))
            ib2 = pe_op(lambda t: (
                t.matmul(psp[0][:], cst["ones_rf"][:], s2row[:, 0:TG],
                         start=True, stop=True),
                t.matmul(psp[1][:], cst["ones_rf"][:], s2row[:, TG:TPC],
                         start=True, stop=True).then_inc(pe, 1)), W(vv, id_s2[7]))
            is2a = dve(lambda v: v.tensor_copy(s2b[:, 0:TG], psp[0][:]).then_inc(vv, 1),
                       W(pe, ib2))
            is2 = dve(lambda v: v.tensor_copy(s2b[:, TG:TPC], psp[1][:]).then_inc(vv, 1))

            wbl = ctx.wgl
            psl = {(p_, t_): is2 for p_ in range(2) for t_ in range(2)}
            tql = [0, 0]

            def proj(tag, n_oc, n_kc, w_dram, w_rows, rhs3, evac, col_off=0,
                     wbuf=None):
                wbuf = wbuf or wbA
                for oc in range(n_oc):
                    par = oc % 2
                    if wbl[par]:
                        Qg(W(pe, wbl[par]))
                    Qg(lambda g, oc=oc, par=par, wbuf=wbuf: g.dma_start(
                        wbuf[par][:, 0:n_kc * 128].rearrange(
                            "p (kc o) -> p kc o", kc=n_kc),
                        w_dram[w_rows:w_rows + n_kc * 128,
                               col_off + oc * 128:col_off + (oc + 1) * 128].rearrange(
                            "(kc p) o -> p kc o", p=128)).then_inc(wg[par], 16))
                    n["wg"][par] += 16
                    waits = [W(wg[par], n["wg"][par])]
                    for tg in range(2):
                        if psl.get((par, tg)):
                            waits.append(W(vv, psl[(par, tg)]))

                    def mm(t, oc=oc, par=par, wbuf=wbuf):
                        w3 = wbuf[par][:, 0:n_kc * 128].rearrange(
                            "p (kc o) -> p kc o", kc=n_kc)
                        for tg in range(2):
                            for kc in range(n_kc):
                                ins = t.matmul(psp[2 * par + tg][:], w3[:, kc, :],
                                               rhs3[:, kc, ts(tg, TG)],
                                               start=(kc == 0), stop=(kc == n_kc - 1))
                        ins.then_inc(pe, 1)
                    mid = pe_op(mm, *waits)
                    wbl[par] = mid
                    for tg in range(2):
                        psl[(par, tg)] = evac(oc, par, tg, mid)

            def ev_fc1(oc, par, tg, mid):
                return dve(lambda v, oc=oc, par=par, tg=tg: v.tensor_tensor(
                    lt3f[:, oc, ts(tg, TG)], psp[2 * par + tg][:],
                    s2b[:, ts(tg, TG)], ALU.mult).then_inc(vv, 1),
                    W(pe, mid), W(vv, is2))
            proj("fc1", KCL, KC, fc1_i, 0, xb3, ev_fc1)

            lat_ready = [psl[(p_, t_)] for p_ in range(2) for t_ in range(2)]
            def ev_exp(oc, par, tg, mid):
                return dve(lambda v, oc=oc, par=par, tg=tg: v.tensor_copy(
                    rt3[:, oc, ts(tg, TG)], psp[2 * par + tg][:]).then_inc(vv, 1),
                    W(pe, mid))
            for i_ in lat_ready:
                ctx.Qt(W(vv, i_))
            proj("exp", KCL, KCL, exp_i, 0, lt3f, ev_exp)

            def ev_up(oc, par, tg, mid):
                g = [W(pe, mid)]
                if tql[par]:
                    g.append(W(aa, tql[par]))
                if tg == 0 and oc >= 2:
                    g.append(W(st[par], n["st"][par]))
                ir = dve(lambda v, par=par, tg=tg: v.tensor_scalar_max(
                    tno[par][:], psp[2 * par + tg][:], 0.0).then_inc(vv, 1), *g)
                iq = act(lambda s, par=par, tg=tg: s.activation(
                    hstg[par][:, ts(tg, TG)], tno[par][:],
                    AF.Square).then_inc(aa, 1), W(vv, ir))
                tql[par] = iq
                if tg == 1:
                    Qs(W(aa, iq))
                    Qs(lambda sy, oc=oc, par=par: sy.dma_start(
                        h_dr[oc * 128:(oc + 1) * 128, :],
                        hstg[par][:]).then_inc(st[par], 16))
                    n["st"][par] += 16
                return ir
            proj("up", INT // 128, KC, up_i, 0, xb3, ev_up)

        with ExitStack() as pxb:
            pb = pxb.enter_context
            t3b = pb(nc.sbuf_tensor(_u + "mo_t3", [128, KC * TPC], BF16))
            hqb = pb(nc.sbuf_tensor(_u + "mo_hq", [128, KCL * TPC], BF16))
            xtile = [pb(nc.sbuf_tensor(_u + f"mo_xt{i}", [128, TG], F32)) for i in range(2)]
            ybf = [pb(nc.sbuf_tensor(_u + f"mo_yb{i}", [128, TG], BF16)) for i in range(2)]
            ysq = [pb(nc.sbuf_tensor(_u + f"mo_yq{i}", [128, TG], BF16)) for i in range(2)]
            tn2 = [pb(nc.sbuf_tensor(_u + f"mo_t2{i}", [128, TG], F32)) for i in range(2)]
            ostg = [pb(nc.sbuf_tensor(_u + f"mo_os{i}", [128, TG], F32)) for i in range(2)]
            wbF = [pb(nc.sbuf_tensor(_u + f"mo_wF{i}", [128, KCL * 128], BF16))
                   for i in range(2)]
            t33 = t3b[:].rearrange("p (kc t) -> p kc t", kc=KC)
            hq3 = hqb[:].rearrange("p (kc t) -> p kc t", kc=KCL)

            wbl = ctx.wgl
            rt_ready = [psl[(p_, t_)] for p_ in range(2) for t_ in range(2)]
            tst = {}
            tadd = {}
            hq_ld = 0
            for hp in range(4):
                g = []
                if hp > 0:
                    g.append(W(pe, wbl[0]))
                    g.append(W(pe, wbl[1]))
                else:
                    g += [W(st[0], n["st"][0]), W(st[1], n["st"][1])]
                for w_ in g:
                    Qs(w_)
                Qs(lambda sy, hp=hp: sy.dma_start(
                    hq3, h_dr[hp * LAT:(hp + 1) * LAT, :].rearrange(
                        "(kc p) t -> p kc t", p=128)).then_inc(ls, 16))
                n["ls"] += 16
                hq_ld = n["ls"]

                for oc in range(KC):
                    par = oc % 2
                    if wbl[par]:
                        Qg(W(pe, wbl[par]))
                    Qg(lambda g_, oc=oc, par=par, hp=hp: g_.dma_start(
                        wbA[par][:, 0:KCL * 128].rearrange(
                            "p (kc o) -> p kc o", kc=KCL),
                        down_i[hp * LAT:(hp + 1) * LAT,
                               oc * 128:(oc + 1) * 128].rearrange(
                            "(kc p) o -> p kc o", p=128)).then_inc(wg[par], 16))
                    n["wg"][par] += 16
                    if hp == 3:
                        Qg(lambda g_, oc=oc, par=par: g_.dma_start(
                            wbF[par][:].rearrange("p (kc o) -> p kc o", kc=KCL),
                            fc2_i[:, oc * 128:(oc + 1) * 128].rearrange(
                                "(kc p) o -> p kc o", p=128)).then_inc(wg[par], 16))
                        n["wg"][par] += 16

                    waits = [W(wg[par], n["wg"][par]), W(ls, hq_ld)]
                    for tg in range(2):
                        if psl.get((par, tg)):
                            waits.append(W(vv, psl[(par, tg)]))
                    if hp == 3 and oc < 2:
                        waits += [W(vv, i_) for i_ in rt_ready]

                    def mm(t, oc=oc, par=par, hp=hp):
                        w3 = wbA[par][:, 0:KCL * 128].rearrange(
                            "p (kc o) -> p kc o", kc=KCL)
                        for tg in range(2):
                            for kc in range(KCL):
                                ins = t.matmul(psp[2 * par + tg][:], w3[:, kc, :],
                                               hq3[:, kc, ts(tg, TG)],
                                               start=(kc == 0),
                                               stop=(kc == KCL - 1 and hp != 3))
                            if hp == 3:
                                wf = wbF[par][:].rearrange("p (kc o) -> p kc o",
                                                           kc=KCL)
                                for kc in range(KCL):
                                    ins = t.matmul(psp[2 * par + tg][:], wf[:, kc, :],
                                                   rt3[:, kc, ts(tg, TG)],
                                                   start=False, stop=(kc == KCL - 1))
                        ins.then_inc(pe, 1)
                    mid = pe_op(mm, *waits)
                    wbl[par] = mid

                    for tg in range(2):
                        if hp == 0:
                            iv_ = dve(lambda v, oc=oc, par=par, tg=tg: v.tensor_copy(
                                t33[:, oc, ts(tg, TG)],
                                psp[2 * par + tg][:]).then_inc(vv, 1), W(pe, mid))
                        else:
                            g2 = [W(pe, mid)]
                            if tadd.get((oc, tg)):
                                g2.append(W(vv, tadd[(oc, tg)]))
                            iv_ = dve(lambda v, oc=oc, par=par, tg=tg: v.tensor_tensor(
                                t33[:, oc, ts(tg, TG)], psp[2 * par + tg][:],
                                t33[:, oc, ts(tg, TG)], ALU.add).then_inc(vv, 1),
                                *g2)
                        tadd[(oc, tg)] = iv_
                        psl[(par, tg)] = iv_
                        if hp == 3:
                            iq = act(lambda s, oc=oc, tg=tg: s.activation(
                                ysq[tg][:], t33[:, oc, ts(tg, TG)],
                                AF.Square).then_inc(aa, 1), W(vv, iv_),
                                *([W(pe, tst[(oc - 1, tg)])] if oc >= 1 else []))
                            def smm(t, oc=oc, tg=tg):
                                t.matmul(pss[0:1, ts(tg, TG)], cst["ones_bf"][:],
                                         t33[:, oc, ts(tg, TG)],
                                         start=(oc == 0), stop=(oc == KC - 1))
                                t.matmul(psq[0:1, ts(tg, TG)], cst["ones_bf"][:],
                                         ysq[tg][:], start=(oc == 0),
                                         stop=(oc == KC - 1)).then_inc(pe, 1)
                            tst[(oc, tg)] = pe_op(smm, W(vv, iv_), W(aa, iq))

            def rows_pass(dst_m, dst_r, wait_pe_cnt):
                out_ids = []
                for tg in range(2):
                    im = dve(lambda v, tg=tg: v.tensor_scalar(
                        row(R_M), pss[0:1, ts(tg, TG)], 1.0 / D, None,
                        ALU.mult).then_inc(vv, 1), W(pe, wait_pe_cnt))
                    i2 = dve(lambda v: v.tensor_tensor(
                        row(R_V), row(R_M), row(R_M), ALU.mult).then_inc(vv, 1),
                        W(vv, im))
                    iv_ = dve(lambda v, tg=tg: v.scalar_tensor_tensor(
                        row(R_V), psq[0:1, ts(tg, TG)], 1.0 / D, row(R_V),
                        ALU.mult, ALU.subtract).then_inc(vv, 1), W(vv, i2))
                    isd = act(lambda s: s.activation(
                        row(R_S), row(R_V), AF.Sqrt,
                        bias=cst["eps"][0:1, 0:1]).then_inc(aa, 1), W(vv, iv_))
                    ir = dve(lambda v: v.reciprocal(row(R_R),
                                                    row(R_S)).then_inc(vv, 1),
                             W(aa, isd))
                    ibc = pe_op(lambda t, tg=tg: (
                        t.matmul(psp[0][:], cst["ones_rf"][:], row(R_M),
                                 start=True, stop=True),
                        t.matmul(psp[1][:], cst["ones_rf"][:], row(R_R),
                                 start=True, stop=True).then_inc(pe, 1)),
                        W(vv, ir))
                    imb = dve(lambda v, tg=tg: v.tensor_copy(
                        dst_m[:, ts(tg, TG)], psp[0][:]).then_inc(vv, 1),
                        W(pe, ibc))
                    irb = dve(lambda v, tg=tg: v.tensor_copy(
                        dst_r[:, ts(tg, TG)], psp[1][:]).then_inc(vv, 1))
                    out_ids.append(irb)
                return out_ids[-1]

            for k_ in psl:
                ctx.Qv(W(vv, psl[k_]))
            irb_i = rows_pass(mb_sb, rb_sb, tst[(KC - 1, 1)])

            xtl = [0, 0]

            def lnout(oc, par, tg, dep, xp=0):
                i1 = dve(lambda v, oc=oc, par=par, tg=tg: v.tensor_tensor(
                    tno[par][:], t33[:, oc, ts(tg, TG)], mb_sb[:, ts(tg, TG)],
                    ALU.subtract).then_inc(vv, 1), *dep)
                i2 = dve(lambda v, par=par, tg=tg: v.tensor_tensor(
                    tno[par][:], tno[par][:], rb_sb[:, ts(tg, TG)],
                    ALU.mult).then_inc(vv, 1), W(vv, i1))
                i3 = dve(lambda v, oc=oc, par=par: v.tensor_scalar(
                    tno[par][:], tno[par][:], gam_sb[:, oc:oc + 1],
                    bet_sb[:, oc:oc + 1], ALU.mult, ALU.add).then_inc(vv, 1),
                    W(vv, i2))
                i4 = dve(lambda v, par=par, xp=xp: v.tensor_tensor(
                    tn2[par][:], tno[par][:], xtile[xp][:],
                    ALU.add).then_inc(vv, 1), W(vv, i3),
                    W(lx[xp], n["lx"][xp]))
                xtl[xp] = i4
                return i4

            ust = {}
            for oc in range(KC):
                for tg in range(2):
                    par = (2 * oc + tg) % 2
                    if xtl[par]:
                        Qs(W(vv, xtl[par]))
                    Qs(lambda sy, oc=oc, par=par, tg=tg: sy.dma_start(
                        xtile[par][:],
                        x3_ap[oc * 128:(oc + 1) * 128, ts(tg, TG)],
                    ).then_inc(lx[par], 16))
                    n["lx"][par] += 16
                    iu = lnout(oc, par, tg, [W(vv, irb_i)], xp=par)
                    waits = [W(vv, iu)]
                    if ust.get((oc - 1, tg, "u")):
                        waits.append(W(pe, ust[(oc - 1, tg, "u")]))
                    ib = dve(lambda v, par=par: v.tensor_copy(
                        ybf[par][:], tn2[par][:]).then_inc(vv, 1), *waits)
                    iq = act(lambda s, par=par: s.activation(
                        ysq[par][:], ybf[par][:], AF.Square).then_inc(aa, 1),
                        W(vv, ib))

                    def smm(t, oc=oc, tg=tg, par=par):
                        t.matmul(pss[0:1, ts(tg, TG)], cst["ones_bf"][:],
                                 ybf[par][:], start=(oc == 0), stop=(oc == KC - 1))
                        t.matmul(psq[0:1, ts(tg, TG)], cst["ones_bf"][:],
                                 ysq[par][:], start=(oc == 0),
                                 stop=(oc == KC - 1)).then_inc(pe, 1)
                    g = [W(vv, ib), W(aa, iq)]
                    if oc == 0:
                        g.append(W(vv, irb_i))
                    ust[(oc, tg, "u")] = pe_op(smm, *g)

            irb_o = rows_pass(mb2_sb, rb2_sb, ust[(KC - 1, 1, "u")])

            for oc in range(KC):
                for tg in range(2):
                    par = (2 * oc + tg) % 2
                    if xtl[par]:
                        Qs(W(vv, xtl[par]))
                    Qs(lambda sy, oc=oc, par=par, tg=tg: sy.dma_start(
                        xtile[par][:],
                        x3_ap[oc * 128:(oc + 1) * 128, ts(tg, TG)],
                    ).then_inc(lx[par], 16))
                    n["lx"][par] += 16
                    iu = lnout(oc, par, tg, [W(vv, irb_o)], xp=par)
                    i5 = dve(lambda v, par=par, tg=tg: v.tensor_tensor(
                        tno[par][:], tn2[par][:], mb2_sb[:, ts(tg, TG)],
                        ALU.subtract).then_inc(vv, 1), W(vv, iu))
                    sg = [W(st[par], n["st"][par])] if n["st"][par] > st_base[par] \
                        else []
                    i6 = dve(lambda v, par=par, tg=tg: v.tensor_tensor(
                        ostg[par][:], tno[par][:], rb2_sb[:, ts(tg, TG)],
                        ALU.mult).then_inc(vv, 1), W(vv, i5), *sg)
                    Qs(W(vv, i6))
                    Qs(lambda sy, oc=oc, par=par, tg=tg: sy.dma_start(
                        out_ap[oc * 128:(oc + 1) * 128, ts(tg, TG)],
                        ostg[par][:]).then_inc(st[par], 16))
                    n["st"][par] += 16

        t0, t1 = n["st"][0], n["st"][1]
        ctx.Qs(lambda sy: sy.wait_ge(st[0], t0))
        ctx.Qs(lambda sy: sy.wait_ge(st[1], t1))


def build_mega(upto=7):
    nc = bass.Bass("TRN2", target_bir_lowering=False)
    DKV = NKV * HEAD
    xt_i = nc.dram_tensor("xt", [D, TPC], F32, kind="ExternalInput").ap()
    wq0 = nc.dram_tensor("w_q0", [D, D], F32, kind="ExternalInput").ap()
    wk0 = nc.dram_tensor("w_k0", [D, DKV], F32, kind="ExternalInput").ap()
    wv0 = nc.dram_tensor("w_v0", [D, DKV], F32, kind="ExternalInput").ap()
    wo0 = nc.dram_tensor("w_o0", [D, D], F32, kind="ExternalInput").ap()
    w2T = nc.dram_tensor("w2T", [D, D], I32, kind="ExternalInput").ap()
    wsc = nc.dram_tensor("wscale", [1, 1], F32, kind="ExternalInput").ap()
    bias0 = nc.dram_tensor("bias0", [D], F32, kind="ExternalInput").ap()
    wqkv = nc.dram_tensor("w_qkv", [D, D + 2 * DKV], F32, kind="ExternalInput").ap()
    wo1 = nc.dram_tensor("w_o1", [D, D], F32, kind="ExternalInput").ap()
    gate_i = nc.dram_tensor("gate_w", [D, 8], F32, kind="ExternalInput").ap()
    up_i = nc.dram_tensor("up_w", [D, 8192], F32, kind="ExternalInput").ap()
    down_i = nc.dram_tensor("down_w", [8192, D], F32, kind="ExternalInput").ap()
    fc1_i = nc.dram_tensor("fc1_w", [D, 2048], F32, kind="ExternalInput").ap()
    fc2_i = nc.dram_tensor("fc2_w", [2048, D], F32, kind="ExternalInput").ap()
    exp_i = nc.dram_tensor("expert_w", [2048, 2048], F32, kind="ExternalInput").ap()
    gam_i = nc.dram_tensor("ln_gamma", [D], F32, kind="ExternalInput").ap()
    bet_i = nc.dram_tensor("ln_beta", [D], F32, kind="ExternalInput").ap()
    msk0_i = nc.dram_tensor("masks0", [128, 8 * TG], BF16, kind="ExternalInput").ap()
    sd1_i = nc.dram_tensor("sd1", [128, 16 * TG], BF16, kind="ExternalInput").ap()
    isl_i = nc.dram_tensor("islopes", [128, 32 * 128], BF16, kind="ExternalInput").ap()
    out_o = nc.dram_tensor("out", [D, TPC], F32, kind="ExternalOutput").ap()

    q_dr = nc.dram_tensor("s_q", [D, TPC], BF16, kind="Internal").ap()
    at_dr = nc.dram_tensor("s_at", [D, TPC], BF16, kind="Internal").ap()
    x2_dr = nc.dram_tensor("s_x2", [D, TPC], F32, kind="Internal").ap()
    x3_dr = nc.dram_tensor("s_x3", [D, TPC], F32, kind="Internal").ap()
    x3b_dr = nc.dram_tensor("s_x3b", [D, TPC], BF16, kind="Internal").ap()
    h_dr = nc.dram_tensor("s_h", [8192, TPC], BF16, kind="Internal").ap()
    ck0 = nc.dram_tensor("s_ck0", [DKV, TPC], BF16, kind="Internal")
    cv0 = nc.dram_tensor("s_cv0", [TPC, DKV], BF16, kind="Internal")
    gk0 = nc.dram_tensor("s_gk0", [2 * DKV, TPC], BF16, kind="Internal")
    gv0 = nc.dram_tensor("s_gv0", [2 * TPC, DKV], BF16, kind="Internal")
    ck1 = nc.dram_tensor("s_ck1", [DKV, TPC], BF16, kind="Internal")
    cv1 = nc.dram_tensor("s_cv1", [TPC, DKV], BF16, kind="Internal")
    gk1 = nc.dram_tensor("s_gk1", [2 * DKV, TPC], BF16, kind="Internal")
    gv1 = nc.dram_tensor("s_gv1", [2 * TPC, DKV], BF16, kind="Internal")

    ident_f_d = nc.inline_tensor(np.eye(128, dtype=np.float32), "mident_f").ap()
    ident_b_d = nc.inline_tensor(np.eye(128, dtype=ml_dtypes.bfloat16), "mident_b").ap()

    with ExitStack() as ctxs:
        e = ctxs.enter_context
        ctx = MCtx(nc, e)
        cst = {
            "ones_bf": e(nc.sbuf_tensor("c_ones_bf", [128, 1], BF16)),
            "ones_rf": e(nc.sbuf_tensor("c_ones_rf", [1, 128], F32)),
            "ones_rb": e(nc.sbuf_tensor("c_ones_rb", [1, 128], BF16)),
            "eps": e(nc.sbuf_tensor("c_eps", [1, 1], F32)),
            "ident_f": e(nc.sbuf_tensor("c_id_f", [128, 128], F32)),
            "ident_bf": e(nc.sbuf_tensor("c_id_b", [128, 128], BF16)),
            "ones_bb": e(nc.sbuf_tensor("c_ones_bb", [128, 128], BF16)),
        }
        blk = e(nc.Block())

        for t_, v_ in [("ones_bf", 1.0), ("ones_rf", 1.0), ("ones_rb", 1.0),
                       ("eps", 1e-5), ("ones_bb", 1.0)]:
            ctx.Qg(lambda g, t_=t_, v_=v_: g.memset(cst[t_][:], v_).then_inc(ctx.cb, 1))
            ctx.n["cb"] += 1
        ctx.Qg(lambda g: g.dma_start(cst["ident_f"][:], ident_f_d).then_inc(ctx.lg, 16))
        ctx.Qg(lambda g: g.dma_start(cst["ident_bf"][:], ident_b_d).then_inc(ctx.lg, 16))
        ctx.n["lg"] += 32
        CB0, LG0 = ctx.n["cb"], ctx.n["lg"]
        ctx.Qg(lambda g: g.wait_ge(ctx.cb, CB0))
        ctx.Qg(lambda g: g.wait_ge(ctx.lg, LG0))
        ctx.barrier()

        cs_t = {}
        emit_qkv_phase(ctx, cst, xt_i, True, wq0, wk0, wv0, q_dr,
                       ck0.ap(), cv0.ap(),
                       ag=lambda cbkv: cs_t.__setitem__(
                           0, emit_allgather(ctx, ck0, cv0, gk0, gv0, cbkv,
                                             barrier=False)))
        if upto == 1:
            tgt = ctx.n["cs"]
            ctx.Qg(lambda g: g.wait_ge(ctx.cs, tgt))
            ctx.barrier()
        if upto >= 2:
            emit_attn_phase(ctx, cst, 0, q_dr, gk0.ap(), gv0.ap(),
                            msk0_i, None, at_dr, cs_wait=cs_t[0])
        if upto >= 3:
            emit_postattn_phase(ctx, cst, 0, at_dr, xt_i, wo0,
                                (w2T, wsc, bias0), (x2_dr,))
        if upto >= 4:
            emit_qkv_phase(ctx, cst, x2_dr, True, wqkv[:, 0:D],
                           wqkv[:, D:D + DKV], wqkv[:, D + DKV:D + 2 * DKV],
                           q_dr, ck1.ap(), cv1.ap(),
                           ag=lambda cbkv: cs_t.__setitem__(
                               1, emit_allgather(ctx, ck1, cv1, gk1, gv1, cbkv,
                                                 barrier=False)))
        if upto >= 5:
            emit_attn_phase(ctx, cst, 1, q_dr, gk1.ap(), gv1.ap(),
                            sd1_i, isl_i, at_dr, cs_wait=cs_t[1])
        if upto >= 6:
            emit_postattn_phase(ctx, cst, 1, at_dr, x2_dr, wo1, None,
                                (x3_dr, x3b_dr))
        if upto >= 7:
            emit_moe_phase(ctx, cst, x3_dr, x3b_dr, gate_i, up_i, down_i,
                           fc1_i, fc2_i, exp_i, gam_i, bet_i, h_dr, out_o)

        if upto < 7:
            src = {1: gk0.ap(), 2: at_dr, 3: x2_dr, 4: gk1.ap(),
                   5: at_dr, 6: x3_dr}[upto]
            ctx.Qg(lambda g: g.dma_start(
                out_o[0:2048 if upto in (1, 4) else D, :],
                src[0:2048 if upto in (1, 4) else D, :]).then_inc(ctx.lg, 16))
            ctx.n["lg"] += 16
            LGD = ctx.n["lg"]
            ctx.Qg(lambda g: g.wait_ge(ctx.lg, LGD))

        ctx.replay(blk)
    return nc


def _mega_masks(p):
    m = np.zeros((8, 128, 512), dtype=np.float32)
    for i in range(8):
        for jj in range(4):
            v = 2 * jj + p - i
            blkv = m[i, :, jj * 128:(jj + 1) * 128]
            if v == 0:
                kk = np.arange(128)
                blkv[:] = np.where(kk[None, :] >= kk[:, None], 0.0, NEG)
            elif v < 0:
                blkv[:] = NEG
    return m.reshape(8, 128, 512).transpose(1, 0, 2).reshape(128, 8 * 512) \
        .astype(BF16NP)


def _mega_sd(p):
    m = np.zeros((16, 128, 512), dtype=np.float32)
    for dl in range(16):
        delta = dl - 7
        for jj in range(4):
            rel0 = (delta + 2 * jj + p) * 128
            kk = np.arange(128)
            rel = rel0 + kk[None, :] - kk[:, None]
            m[dl, :, jj * 128:(jj + 1) * 128] = np.where(
                rel >= 0, np.sqrt(np.maximum(rel, 0.0)), 4.0e30)
    return m.transpose(1, 0, 2).reshape(128, 16 * 512).astype(BF16NP)


def _mega_islopes():
    sl = _alibi_slopes(32).astype(np.float64)
    out = np.zeros((128, 32 * 128), dtype=np.float32)
    for h in range(32):
        np.fill_diagonal(out[:, h * 128:(h + 1) * 128],
                         -sl[h] * math.sqrt(HEAD))
    return out.astype(BF16NP)


def kernel_fused(L0_attn_hidden_states, L0_attn_w_q, L0_attn_w_k, L0_attn_w_v,
                 L0_attn_w_o, L0_ffn_weight_2bit, L0_ffn_weight_scale,
                 L0_ffn_bias, L1_attn_w_qkv, L1_attn_w_o, L1_ffn_gate_w,
                 L1_ffn_up_w, L1_ffn_down_w, L1_ffn_fc1_w, L1_ffn_fc2_w,
                 L1_ffn_expert_w, L1_ffn_ln_gamma, L1_ffn_ln_beta, upto=7):
    x = np.asarray(L0_attn_hidden_states)
    f = np.asarray
    w2t = np.ascontiguousarray(np.asarray(L0_ffn_weight_2bit).T)
    nc = _get_nc(("mega", upto)) if ("mega", upto) in _NC_CACHE else None
    if nc is None:
        _NC_CACHE[("mega", upto)] = build_mega(upto)
        nc = _NC_CACHE[("mega", upto)]
    in_maps = []
    for c in range(NCORES):
        b, p = c // 2, c % 2
        xs = x[b].reshape(16, 128, D)[p::2].reshape(TPC, D)
        in_maps.append({
            "xt": np.ascontiguousarray(xs.T),
            "w_q0": f(L0_attn_w_q), "w_k0": f(L0_attn_w_k),
            "w_v0": f(L0_attn_w_v), "w_o0": f(L0_attn_w_o),
            "w2T": w2t, "wscale": f(L0_ffn_weight_scale).reshape(1, 1),
            "bias0": f(L0_ffn_bias),
            "w_qkv": f(L1_attn_w_qkv), "w_o1": f(L1_attn_w_o),
            "gate_w": f(L1_ffn_gate_w), "up_w": f(L1_ffn_up_w),
            "down_w": f(L1_ffn_down_w), "fc1_w": f(L1_ffn_fc1_w),
            "fc2_w": f(L1_ffn_fc2_w), "expert_w": f(L1_ffn_expert_w),
            "ln_gamma": f(L1_ffn_ln_gamma), "ln_beta": f(L1_ffn_ln_beta),
            "masks0": _mega_masks(p), "sd1": _mega_sd(p),
            "islopes": _mega_islopes(),
        })
    res = run_bass_kernel_spmd(nc, in_maps, core_ids=list(range(NCORES)))
    out = np.empty((B, S, D), dtype=np.float32)
    for c in range(NCORES):
        b, p = c // 2, c % 2
        ob = np.asarray(res.results[c]["out"]).T.reshape(8, 128, D)
        full = out[b].reshape(16, 128, D)
        full[p::2] = ob
    return out


def kernel(**inputs):
    return kernel_fused(**inputs)
